# version 14 (final)
# baseline (speedup 1.0000x reference)
import numpy as np
import ml_dtypes
from contextlib import ExitStack
import concourse.bass as bass
import concourse.mybir as mybir
from concourse.bass_utils import run_bass_kernel_spmd

F32 = mybir.dt.float32
BF16 = mybir.dt.bfloat16
AF = mybir.ActivationFunctionType
ALU = mybir.AluOpType
AX = mybir.AxisListType
bfdt = ml_dtypes.bfloat16

T = 2048
D = 1024
TT = T // 128
FC = D // 128
NEG = -1.0e30
LN_EPS = 1e-5

ENGS = ("tensor", "vector", "scalar", "gpsimd", "sync")


class Phaser:
    def __init__(self, nc, es):
        self.nc = nc
        self.SP = es.enter_context(nc.semaphore("SP"))
        self._dma_sems = {}
        self._dma_counts = {}
        self._es = es
        self.phases = []
        self.cur = {e: [] for e in ENGS}
        self.cur_needs = []

    def dma_count(self, name):
        return self._dma_counts.get(name, 0)

    def _sem(self, name):
        if name not in self._dma_sems:
            self._dma_sems[name] = self._es.enter_context(self.nc.semaphore(name))
            self._dma_counts[name] = 0
        return self._dma_sems[name]

    def op(self, eng, fn):
        self.cur[eng].append(("op", fn))

    def dma(self, eng, sem_name, fn):
        sem = self._sem(sem_name)
        self._dma_counts[sem_name] += 16
        self.cur[eng].append(("dma", fn, sem))
        return self._dma_counts[sem_name]

    def need(self, sem_name, target):
        if target > 0:
            self._sem(sem_name)
            self.cur_needs.append((sem_name, target))

    def phase(self):
        if any(self.cur[e] for e in ENGS):
            self.phases.append((self.cur, self.cur_needs))
            self.cur = {e: [] for e in ENGS}
            self.cur_needs = []
        else:
            assert not self.cur_needs, "needs declared in empty phase"

    def replay(self):
        self.phase()
        barrier_before = []
        tot = 0
        for ops, _ in self.phases:
            barrier_before.append(tot)
            tot += sum(1 for e in ENGS if ops[e])
        nc = self.nc
        with nc.Block() as blk:
            for eng_name in ENGS:
                deco = getattr(blk, eng_name)

                def make_body(ename):
                    def body(eng):
                        for pi, (ops, needs) in enumerate(self.phases):
                            items = ops[ename]
                            if not items:
                                continue
                            if barrier_before[pi] > 0:
                                eng.wait_ge(self.SP, barrier_before[pi])
                            for sem_name, tgt in needs:
                                eng.wait_ge(self._dma_sems[sem_name], tgt)
                            inst = None
                            last_was_dma = True
                            for item in items:
                                if item[0] == "op":
                                    inst = item[1](eng)
                                    last_was_dma = False
                                else:
                                    di = item[1](eng)
                                    di.then_inc(item[2], 16)
                                    last_was_dma = True
                            if last_was_dma or inst is None:
                                eng.sem_inc(self.SP, 1)
                            else:
                                inst.then_inc(self.SP, 1)
                    return body

                deco(make_body(eng_name))


class Arena:
    def __init__(self, ap, name):
        self.ap = ap
        self.n = ap.shape[1]
        self.off = 0
        self.name = name
        self.marks = []

    def take(self, cols):
        assert self.off + cols <= self.n, (
            f"arena {self.name} overflow: {self.off}+{cols} > {self.n}")
        sl = self.ap[:, self.off:self.off + cols]
        self.off += cols
        return sl

    def take3(self, mid, inner):
        return self.take(mid * inner).rearrange("p (a b) -> p a b", a=mid)

    def push(self):
        self.marks.append(self.off)

    def pop(self):
        self.off = self.marks.pop()


class B:
    DBG_NO_EVAC = False
    DBG_NO_TRANS = False
    DBG_NQT = 4
    DBG_NO_XADD = False
    DBG_QT1_UPTO = None
    DBG_NO_CAST = False
    DBG_NO_GP = False
    def __init__(self, nc, es, n_bf=50688, n_f32=768):
        self.nc = nc
        self.P = Phaser(nc, es)
        self.X = nc.alloc_sbuf_tensor("X", [128, TT * D], F32).ap()
        self.xn_fm = nc.alloc_sbuf_tensor("xn_fm", [128, FC * T], BF16).ap()
        self.abf = Arena(nc.alloc_sbuf_tensor("abf", [128, n_bf], BF16).ap(), "bf")
        self.af32 = Arena(nc.alloc_sbuf_tensor("af32", [128, n_f32], F32).ap(), "f32")
        self.ident_bf = nc.alloc_sbuf_tensor("ident_bf", [128, 128], BF16).ap()
        self.ident_f32 = nc.alloc_sbuf_tensor("ident_f32", [128, 128], F32).ap()
        self.mask_w = nc.alloc_sbuf_tensor("mask_w", [128, 256], BF16).ap()
        self.mask256 = nc.alloc_sbuf_tensor("mask256", [128, 256], BF16).ap()
        self.maskpad = nc.alloc_sbuf_tensor("maskpad", [128, 1024], BF16).ap()
        self.ones_bf = nc.alloc_sbuf_tensor("ones_bf", [128, 128], BF16).ap()
        self.ones_f32 = nc.alloc_sbuf_tensor("ones_f32", [1, 128], F32).ap()
        self.onec_f32 = nc.alloc_sbuf_tensor("onec_f32", [128, 1], F32).ap()
        self.pbig = nc.alloc_psum_tensor("pbig", [128, 2048], F32).ap()
        self.pb = ([self.pbig[:, i * 512:(i + 1) * 512] for i in range(4)]
                   + [nc.alloc_psum_tensor(f"pb{i}", [128, 512], F32).ap()
                      for i in (4, 5)])
        self.pt = [nc.alloc_psum_tensor(f"pt{i}", [128, 1024], BF16).ap()
                   for i in range(2)]
        self.ln_rstd = nc.alloc_sbuf_tensor("ln_rstd", [128, TT], F32).ap()
        self.ln_b = nc.alloc_sbuf_tensor("ln_b", [128, TT], F32).ap()
        self.stat = nc.alloc_sbuf_tensor("stat", [128, 64], F32).ap()
        self.xtbuf = nc.alloc_sbuf_tensor("xtbuf", [128, 2, D], BF16).ap()
        self._evac_flip = 0

    @property
    def mask_c(self):
        return self.mask_w[:, 128:256]

    def ptv(self, i):
        return self.pt[i]

    def evac_eng(self):
        self._evac_flip ^= 1
        return "vector" if self._evac_flip else "scalar"

    def _evac(self, e, dst, src, scale=1.0):
        if hasattr(e, "activation"):
            return e.activation(dst, src, AF.Copy, scale=scale)
        if scale == 1.0:
            return e.tensor_copy(dst, src)
        return e.tensor_scalar(dst, src, scale, None, ALU.mult)


    def load_consts(self, cmap):
        P = self.P
        for attr, ap in cmap.items():
            dst = getattr(self, attr)
            P.dma("sync", "dC", lambda e, d=dst, s=ap: e.dma_start(out=d, in_=s))
        self.const_tgt = P.dma_count("dC")

    def load_x(self, x_dram):
        P = self.P
        xr = x_dram.rearrange("(t p) d -> p t d", p=128)
        self.x_tgt = P.dma("sync", "dX", lambda e, s=xr: e.dma_start(
            out=self.X.rearrange("p (t d) -> p t d", t=TT), in_=s))

    def store_x(self, out_dram):
        P = self.P
        P.phase()
        xr = out_dram.rearrange("(t p) d -> p t d", p=128)
        tgt = P.dma("sync", "dOut", lambda e, d=xr: e.dma_start(
            out=d, in_=self.X.rearrange("p (t d) -> p t d", t=TT)))
        P.phase()
        P.need("dOut", tgt)
        P.op("sync", lambda e: e.nop())
        P.phase()

    def transpose_waves(self, items, f32=False):
        P = self.P
        ident = self.ident_f32 if f32 else self.ident_bf
        wave = 4 if f32 else 8
        pend = None
        for c0 in range(0, len(items), wave):
            chunk = items[c0:c0 + wave]
            w = (c0 // wave) % 2
            for i, (src, dst) in enumerate(chunk):
                if f32:
                    slot = self.pb[4 + w][:, i * 128:(i + 1) * 128]
                else:
                    slot = self.ptv(w)[:, i * 128:(i + 1) * 128]
                if not self.DBG_NO_TRANS:
                    P.op("tensor", lambda e, s=src, o=slot, idn=ident:
                         e.transpose(o, s, idn))
            if pend is not None and not self.DBG_NO_EVAC:
                eng = "vector" if pend[1] == 0 else "scalar"
                for i, (src, dst) in enumerate(pend[0]):
                    if f32:
                        slot = self.pb[4 + pend[1]][:, i * 128:(i + 1) * 128]
                    else:
                        slot = self.ptv(pend[1])[:, i * 128:(i + 1) * 128]
                    P.op(eng, lambda e, o=dst, s=slot: e.tensor_copy(o, s)
                         if hasattr(e, "tensor_copy") else e.copy(o, s))
            pend = (chunk, w)
            P.phase()
        if pend is not None:
            eng = "vector" if pend[1] == 0 else "scalar"
            for i, (src, dst) in enumerate(pend[0]):
                if f32:
                    slot = self.pb[4 + pend[1]][:, i * 128:(i + 1) * 128]
                else:
                    slot = self.ptv(pend[1])[:, i * 128:(i + 1) * 128]
                P.op(eng, lambda e, o=dst, s=slot: e.tensor_copy(o, s)
                     if hasattr(e, "tensor_copy") else e.copy(o, s))
            P.phase()

    def emit_ln(self, needs=(), upto="full"):
        P = self.P
        st = self.stat
        ssum = st[:, 0:16]
        ssq = st[:, 16:32]
        for g in range(TT // 2):
            if g == 0:
                for nm, tgt in needs:
                    P.need(nm, tgt)
            for i in range(2):
                t = 2 * g + i
                xt = self.X[:, t * D:(t + 1) * D]
                P.op("vector", lambda e, x=xt, c=ssum[:, t:t + 1]:
                     e.reduce_sum(c, x, axis=AX.X))
                P.op("scalar", lambda e, x=xt, c=ssq[:, t:t + 1], i=i:
                     e.activation(self.xtbuf[:, i, :], x, AF.Square,
                                  accum_out=c))
            P.phase()
        mu = st[:, 32:48]
        var = st[:, 48:64]
        q2 = st[:, 0:16]
        P.op("vector", lambda e: e.tensor_scalar_mul(mu, ssum, 1.0 / D))
        P.op("vector", lambda e: e.tensor_scalar_mul(self.ln_b, ssq, 1.0 / D))
        P.phase()
        P.op("vector", lambda e: e.scalar_tensor_tensor(
            var, mu, -1.0, mu, ALU.mult, ALU.mult))
        P.phase()
        P.op("vector", lambda e: e.scalar_tensor_tensor(
            var, var, LN_EPS, self.ln_b, ALU.add, ALU.add))
        P.phase()
        if upto == "lnstat":
            return
        P.op("scalar", lambda e: e.activation(self.ln_rstd, var, AF.Sqrt))
        P.phase()
        P.op("vector", lambda e: e.reciprocal(self.ln_rstd, self.ln_rstd))
        P.op("vector", lambda e: e.tensor_scalar_mul(self.ln_b, mu, -1.0))
        P.phase()
        if upto == "lncol":
            return
        for g in range(TT // 2):
            for i in range(2):
                t = 2 * g + i
                eng = "vector"
                P.op(eng, lambda e, t=t, i=i: e.tensor_scalar(
                    self.xtbuf[:, i, :], self.X[:, t * D:(t + 1) * D],
                    self.ln_b[:, t:t + 1], self.ln_rstd[:, t:t + 1],
                    ALU.add, ALU.mult))
            P.phase()
            if upto == "lnapply" and g == 0:
                return
            items = []
            for i in range(2):
                t = 2 * g + i
                for fc in range(FC):
                    items.append((self.xtbuf[:, i, fc * 128:(fc + 1) * 128],
                                  self.xn_fm[:, fc * T + t * 128: fc * T + (t + 1) * 128]))
            self.transpose_waves(items)
            if upto == f"lng{g}":
                return

    def xn_chunk(self, ic):
        return self.xn_fm[:, ic * T:(ic + 1) * T]

    def mm_waves(self, groups, evac_engines=None):
        P = self.P
        pend = None
        for c0 in range(0, len(groups), 3):
            chunk = groups[c0:c0 + 3]
            w = (c0 // 3) % 2
            base = 0 if w == 0 else 3
            for i, (mm, _) in enumerate(chunk):
                P.op("tensor", lambda e, f=mm, p=self.pb[base + i]: f(e, p))
            if pend is not None:
                pb0 = 0 if pend[1] == 0 else 3
                for i, (_, ev) in enumerate(pend[0]):
                    eng = (evac_engines[i % len(evac_engines)]
                           if evac_engines else self.evac_eng())
                    P.op(eng, lambda e, f=ev, p=self.pb[pb0 + i]: f(e, p))
            pend = (chunk, w)
            P.phase()
        if pend is not None:
            pb0 = 0 if pend[1] == 0 else 3
            for i, (_, ev) in enumerate(pend[0]):
                eng = (evac_engines[i % len(evac_engines)]
                       if evac_engines else self.evac_eng())
                P.op(eng, lambda e, f=ev, p=self.pb[pb0 + i]: f(e, p))
            P.phase()

    def prep_wt(self, w_dram, scratch, n_in, n_out, tag):
        return self.P.dma("gpsimd", "dW",
                          lambda e, s=w_dram, d=scratch: e.dma_start(out=d, in_=s))

    def read_wt_stripes(self, scratch, wt_sb, n_in, n_out):
        return self.P.dma("sync", "dT", lambda e, s=scratch, d=wt_sb:
                          e.dma_start(out=d,
                                      in_=s.rearrange("(c p) n -> p c n", p=128)))


    def sl7(self, wq, wk, wv, scr, upto="full"):
        nc, P = self.nc, self.P
        wts = [wq, wk, wv]
        if not self.DBG_NO_CAST:
            cast_tgts = [self.prep_wt(wts[i], scr[i], D, D, f"w{i}")
                         for i in range(3)]
        P.phase()
        if upto == "none":
            return
        self.emit_ln(needs=[("dX", self.x_tgt), ("dC", self.const_tgt)],
                     upto=(upto if upto.startswith("ln") else "full"))
        if upto.startswith("ln"):
            return
        self.abf.push()
        wtbuf = self.abf.take3(8, D)
        q_half = self.abf.take3(8, T // 4)
        k_fm = self.abf.take3(8, T)
        v_tok = self.abf.take3(TT, D)
        P_row = self.abf.take(T)
        P_T = self.abf.take3(TT, 128)

        P.need("dW", cast_tgts[2])
        tgt_k = self.read_wt_stripes(scr[1], wtbuf, D, D)
        P.phase()
        self._proj_fm(k_fm, wtbuf, tgt_k, scale=1.0)
        if upto == "kproj":
            return
        tgt_v = self.read_wt_stripes(scr[2], wtbuf, D, D)
        P.phase()
        self._proj_tok(v_tok, wtbuf, tgt_v)
        if upto == "vproj":
            return
        for quar in range(4):
            tgt_q = self.read_wt_stripes(scr[0], wtbuf, D, D)
            P.phase()
            self._proj_fm(q_half, wtbuf, tgt_q, scale=1.0 / 32.0,
                          tqs=[quar], toff=quar * 512)
            if upto == "qproj":
                return
            self._attn_sl7(q_half, k_fm, v_tok, P_row, P_T,
                           range(quar * 4, quar * 4 + self.DBG_NQT), quar * 512)
            if upto == "attn1":
                return
        self.abf.pop()

    def _attn_sl7(self, q_half, k_fm, v_tok, P_row, P_T, qts, toff):
        P = self.P
        st = self.stat
        for qt in qts:
            stop_at = self.DBG_QT1_UPTO if qt > min(qts) else None
            ksp = (qt + 1) * 128
            nb = (ksp + 511) // 512
            dkb = qt // 4
            doff = (qt % 4) * 128
            q0 = qt * 128 - toff
            for kb in range(nb):
                N = min(512, ksp - kb * 512)
                def mk(e, kb=kb, N=N, q0=q0, doff=doff, diag=(kb == dkb)):
                    if diag:
                        e.matmul(self.pb[kb][:, 0:N], self.ident_bf,
                                 self.maskpad[:, 384 - doff:384 - doff + N],
                                 start=True, stop=False)
                    inst = None
                    for ic in range(FC):
                        inst = e.matmul(
                            self.pb[kb][:, 0:N],
                            q_half[:, ic, q0:q0 + 128],
                            k_fm[:, ic, kb * 512:kb * 512 + N],
                            start=(ic == 0 and not diag),
                            stop=(ic == FC - 1))
                    return inst
                P.op("tensor", mk)
            P.phase()
            if stop_at == "s":
                return
            P.op("vector", lambda e, ksp=ksp: e.reduce_max(
                st[:, 9:10], self.pbig[:, 0:ksp], axis=AX.X, negate=True))
            P.phase()
            if stop_at == "stats":
                return
            P.op("scalar", lambda e, ksp=ksp: e.activation(
                P_row[:, 0:ksp], self.pbig[:, 0:ksp], AF.Exp,
                bias=st[:, 9:10], scale=1.0, accum_out=st[:, 24:25]))
            P.phase()
            P.op("vector", lambda e: e.reciprocal(st[:, 25:26], st[:, 24:25]))
            P.phase()
            P.op("vector", lambda e, ksp=ksp: e.tensor_scalar(
                P_row[:, 0:ksp], P_row[:, 0:ksp], st[:, 25:26], None, ALU.mult))
            P.phase()
            if stop_at == "norm":
                return
            items = [(P_row[:, kt * 128:(kt + 1) * 128], P_T[:, kt, :])
                     for kt in range(qt + 1)]
            self.transpose_waves(items)
            if stop_at == "trans":
                return
            for dh in range(2):
                def mkpv(e, dh=dh, qt=qt):
                    inst = None
                    for kt in range(qt + 1):
                        inst = e.matmul(self.pb[dh], P_T[:, kt, :],
                                        v_tok[:, kt, dh * 512:(dh + 1) * 512],
                                        start=(kt == 0), stop=(kt == qt))
                    return inst
                P.op("tensor", mkpv)
            P.phase()
            if not (self.DBG_NO_XADD and qt > 0):
                for dh in range(2):
                    xs = self.X[:, qt * D + dh * 512: qt * D + (dh + 1) * 512]
                    P.op("vector", lambda e, xs=xs, dh=dh: e.tensor_tensor(
                        xs, xs, self.pb[dh], ALU.add))
                P.phase()


    def sl2_prep(self, w_fc, w_proj, u, v, fb, gate, scr_fc, scr_pr):
        P = self.P
        self._sl2_small_args = (u, v, fb, gate)
        self.abf.push()
        if True:
            pass
        self.abf.pop()
        self.abf.push()
        self.abf.push()
        W32 = self.abf.take(8192).bitcast(F32)
        T32 = self.abf.take(2048).bitcast(F32)
        SG32 = self.abf.take(2048).bitcast(F32)
        OBF = self.abf.take(1024)
        TBF = self.abf.take3(8, 128)
        st = self.stat
        scol = st[:, 40:41]
        thcol = st[:, 41:42]
        for (wsrc, scrdst, nrows, ncols) in [(w_fc, scr_fc, 4096, 1024),
                                             (w_proj, scr_pr, 1024, 4096)]:
            nch = nrows // 128
            nsub = ncols // 1024
            for ch in range(nch):
                tload = P.dma("sync", "dWL", lambda e, ch=ch, wsrc=wsrc,
                              ncols=ncols: e.dma_start(
                    out=W32[:, 0:ncols],
                    in_=wsrc[ch * 128:(ch + 1) * 128, :]))
                P.phase()
                P.need("dWL", tload)
                P.op("vector", lambda e, ncols=ncols: e.reduce_sum(
                    scol, W32[:, 0:ncols], axis=AX.X, apply_absolute_value=True))
                P.phase()
                P.op("vector", lambda e, n=ncols: e.tensor_scalar(
                    st[:, 42:43], scol, 1.0 / n, 1e-5, ALU.mult, ALU.max))
                P.op("vector", lambda e, n=ncols: e.tensor_scalar(
                    thcol, scol, 0.5 / n, 0.5e-5, ALU.mult, ALU.max))
                P.phase()
                for sub in range(nsub):
                    wsl = W32[:, sub * 1024:(sub + 1) * 1024]
                    P.op("scalar", lambda e, wsl=wsl: e.sign(SG32, wsl))
                    P.op("scalar", lambda e, wsl=wsl: e.activation(
                        T32, wsl, AF.Abs))
                    P.phase()
                    P.op("vector", lambda e: e.scalar_tensor_tensor(
                        T32, T32, thcol, SG32, ALU.is_gt, ALU.mult))
                    P.phase()
                    P.op("vector", lambda e: e.tensor_scalar(
                        OBF, T32, st[:, 42:43], None, ALU.mult))
                    P.phase()
                    self.transpose_waves(
                        [(OBF[:, fc * 128:(fc + 1) * 128], TBF[:, fc, :])
                         for fc in range(FC)])
                    tst = P.dma("sync", "dWS", lambda e, ch=ch, sub=sub,
                                scrdst=scrdst: e.dma_start(
                        out=scrdst.rearrange("(f p) m -> p f m", p=128)[
                            :, sub * 8:sub * 8 + 8, ch * 128:(ch + 1) * 128],
                        in_=TBF))
                    P.phase()
                    P.need("dWS", tst)
                    P.op("sync", lambda e: e.nop())
                    P.phase()
        self.abf.pop()

    def _sl2_small(self):
        P = self.P
        u, v, fb, gate = self._sl2_small_args
        vT = self.abf.take3(8, 16)
        uT = self.abf.take(4096)
        fbT = self.abf.take(16)
        gatecol = self.af32.take(1)
        self.abf.push()
        small = self.abf.take(1024 + 512 + 16 + 8)
        v_sb = small[:, 0:1024]
        u_sb = small[:, 0:512].rearrange("p (a b) -> p a b", a=32)
        fb_sb = small[:, 1024:1040]
        gt_sb = small[:, 1040:1041]
        t_small = None
        P.phase()
        t_small = P.dma("gpsimd", "dS2", lambda e: e.dma_start(
            out=v_sb[0:16, :], in_=v))
        t_small = P.dma("gpsimd", "dS2", lambda e: e.dma_start(
            out=fb_sb[0:16, :], in_=fb))
        t_small = P.dma("gpsimd", "dS2", lambda e: e.dma_start(
            out=gt_sb[0:1, :], in_=gate))
        P.phase()
        P.need("dS2", P.dma_count("dS2"))
        P.need("dC", self.const_tgt)
        for ic in range(FC):
            P.op("tensor", lambda e, ic=ic: e.transpose(
                self.ptv(0)[:, ic * 16:(ic + 1) * 16],
                v_sb[0:16, ic * 128:(ic + 1) * 128], self.ident_bf[0:16, 0:16]))
        P.op("tensor", lambda e: e.transpose(
            self.ptv(1)[0:16, 0:16], fb_sb[0:16, :], self.ident_bf[0:16, 0:16]))
        P.op("tensor", lambda e: e.matmul(
            self.pb[0][0:16, 0:1], self.ones_bf[0:1, 0:16], gt_sb[0:1, :],
            start=True, stop=True))
        P.phase()
        P.op("vector", lambda e: e.tensor_copy(
            vT.rearrange("p a b -> p (a b)"), self.ptv(0)[:, 0:128]))
        P.op("scalar", lambda e: e.copy(fbT[0:16, :], self.ptv(1)[0:16, 0:16]))
        P.op("vector", lambda e: e.tensor_copy(gatecol[0:16, :], self.pb[0][0:16, 0:1]))
        P.phase()
        t_u = P.dma("gpsimd", "dS2", lambda e: e.dma_start(
            out=u_sb, in_=u.rearrange("(c p) e -> p c e", p=128)))
        P.phase()
        P.need("dS2", P.dma_count("dS2"))
        for c in range(32):
            w = c % 2
            P.op("tensor", lambda e, c=c, w=w: e.transpose(
                self.ptv(w)[0:16, (c // 2 % 4) * 128:(c // 2 % 4 + 1) * 128],
                u_sb[:, c, :], self.ident_bf))
            if c % 8 == 7:
                P.phase()
                for cc in range(c - 7, c + 1):
                    ww = cc % 2
                    P.op("vector" if ww == 0 else "scalar",
                         lambda e, cc=cc, ww=ww: e.tensor_copy(
                             uT[0:16, cc * 128:(cc + 1) * 128],
                             self.ptv(ww)[0:16, (cc // 2 % 4) * 128:(cc // 2 % 4 + 1) * 128])
                         if hasattr(e, "tensor_copy") else e.copy(
                             uT[0:16, cc * 128:(cc + 1) * 128],
                             self.ptv(ww)[0:16, (cc // 2 % 4) * 128:(cc // 2 % 4 + 1) * 128]))
                P.phase()
        self.abf.pop()
        return vT, uT, fbT, gatecol

    def sl2(self, scr_fc, scr_pr, first=False):
        P = self.P
        needs = [("dX", self.x_tgt), ("dC", self.const_tgt)] if first else []
        self.emit_ln(needs=needs)
        self.abf.push()
        self.af32.push()
        vT, uT, fbT, gatecol = self._sl2_small()
        r2 = self.abf.take3(32, 512)
        w1buf = [self.abf.take3(8, 128), self.abf.take3(8, 128)]
        w2buf = [self.abf.take(512) for _ in range(4)]
        t1 = self.abf.take(2048)
        t2g = self.abf.take(2048)
        rbuf = [self.abf.take(1024).bitcast(F32),
                self.abf.take(1024).bitcast(F32)]
        for tq in range(4):
            def mkt1(e, tq=tq):
                inst = None
                for ic in range(FC):
                    inst = e.matmul(self.pb[0][0:16, 0:512], vT[:, ic, :],
                                    self.xn_chunk(ic)[:, tq * 512:(tq + 1) * 512],
                                    start=(ic == 0), stop=(ic == FC - 1))
                return inst
            P.op("tensor", mkt1)
            P.phase()
            P.op("vector", lambda e, tq=tq: e.tensor_copy(
                t1[0:16, tq * 512:(tq + 1) * 512], self.pb[0][0:16, 0:512]))
            P.phase()
        for tq in range(4):
            P.op("tensor", lambda e, tq=tq: e.matmul(
                self.pb[0][0:16, 0:512], fbT[0:16, :],
                t1[0:16, tq * 512:(tq + 1) * 512], start=True, stop=True))
            P.phase()
            P.op("vector", lambda e, tq=tq: e.tensor_scalar(
                t2g[0:16, tq * 512:(tq + 1) * 512], self.pb[0][0:16, 0:512],
                gatecol[0:16, :], None, ALU.mult))
            P.phase()
        def issue_w1(oc):
            w = oc % 2
            return P.dma("sync", f"dW1{w}", lambda e, oc=oc, w=w: e.dma_start(
                out=w1buf[w],
                in_=scr_fc.rearrange("(c p) n -> p c n", p=128)[
                    :, :, oc * 128:(oc + 1) * 128]))

        def issue_w2(oc, dh):
            w = oc % 4
            return P.dma("sync", f"dW2{w}", lambda e, oc=oc, dh=dh, w=w:
                         e.dma_start(
                out=w2buf[w], in_=scr_pr[oc * 128:(oc + 1) * 128,
                                         dh * 512:(dh + 1) * 512]))

        for quar in range(4):
            w1tgt = {0: issue_w1(0)}
            P.phase()
            for oc in range(34):
                if oc < 32:
                    pbk = 4 + (oc % 2)
                    P.need(f"dW1{oc % 2}", w1tgt[oc])
                    def mkh(e, oc=oc, quar=quar, pbk=pbk):
                        for ic in range(FC):
                            e.matmul(self.pb[pbk], w1buf[oc % 2][:, ic, :],
                                     self.xn_chunk(ic)[:, quar * 512:(quar + 1) * 512],
                                     start=(ic == 0), stop=False)
                        return e.matmul(
                            self.pb[pbk], uT[0:16, oc * 128:(oc + 1) * 128],
                            t2g[0:16, quar * 512:(quar + 1) * 512],
                            start=False, stop=True)
                    P.op("tensor", mkh)
                    if oc + 1 < 32:
                        w1tgt[oc + 1] = issue_w1(oc + 1)
                if 0 <= oc - 1 < 32:
                    o1 = oc - 1
                    P.op("vector", lambda e, o1=o1: e.tensor_scalar(
                        rbuf[o1 % 2], self.pb[4 + (o1 % 2)], 0.0, None, ALU.max))
                if 0 <= oc - 2 < 32:
                    o2 = oc - 2
                    P.op("scalar", lambda e, o2=o2: e.activation(
                        r2[:, o2, :], rbuf[o2 % 2], AF.Square))
                P.phase()
            for dh in range(2):
                w2tgt = {0: issue_w2(0, dh), 1: issue_w2(1, dh)}
                P.phase()
                for oc0 in range(0, 32, 2):
                    P.need(f"dW2{oc0 % 4}", w2tgt[oc0])
                    P.need(f"dW2{(oc0 + 1) % 4}", w2tgt[oc0 + 1])
                    def mko(e, oc0=oc0):
                        inst = None
                        for oc in (oc0, oc0 + 1):
                            for tl in range(4):
                                inst = e.matmul(
                                    self.pb[tl],
                                    r2[:, oc, tl * 128:(tl + 1) * 128],
                                    w2buf[oc % 4],
                                    start=(oc == 0), stop=(oc == 31))
                        return inst
                    P.op("tensor", mko)
                    for nx in (oc0 + 2, oc0 + 3):
                        if nx < 32:
                            w2tgt[nx] = issue_w2(nx, dh)
                    P.phase()
                for tl in range(4):
                    t = quar * 4 + tl
                    xs = self.X[:, t * D + dh * 512: t * D + (dh + 1) * 512]
                    P.op("vector", lambda e, xs=xs, tl=tl: e.tensor_tensor(
                        xs, xs, self.pb[tl], ALU.add))
                P.phase()
        self.abf.pop()


    def sl1(self, wq, bq, wk, bk, wv, bv, wo, bo, sinks, scr, first=False):
        P = self.P
        for nm, src in (("q", wq), ("k", wk), ("v", wv), ("o", wo)):
            P.dma("gpsimd", "dW1c", lambda e, s=src, d=scr[nm]: e.dma_start(
                out=d, in_=s))
        cast_tgt = P.dma_count("dW1c")
        P.phase()
        needs = [("dX", self.x_tgt), ("dC", self.const_tgt)] if first else []
        self.emit_ln(needs=needs)
        self.abf.push()
        self.af32.push()
        bq8 = self.af32.take(32)
        bkc = self.af32.take(4)
        sinkbc = self.af32.take(64)
        negsinkrep = self.af32.take(256).rearrange(
            "p (j u) -> p j u", j=32)
        bobc = self.abf.take(2048).bitcast(F32)
        bvrow = self.abf.take(512)
        self.abf.push()
        brow = self.abf.take(2688).bitcast(F32)
        tb = P.dma("sync", "dB1", lambda e: e.dma_start(
            out=brow[0:32, 0:128], in_=bq.rearrange("(c p) -> c p", p=128)))
        tb = P.dma("sync", "dB1", lambda e: e.dma_start(
            out=brow[0:4, 128:256], in_=bk.rearrange("(c p) -> c p", p=128)))
        tb = P.dma("sync", "dB1", lambda e: e.dma_start(
            out=brow[0:1, 256:320], in_=sinks.rearrange("(o s) -> o s", o=1)))
        tb = P.dma("sync", "dB1", lambda e: e.dma_start(
            out=brow[0:1, 320:1344], in_=bo.rearrange("(o s) -> o s", o=1)))
        tb = P.dma("gpsimd", "dB1g", lambda e: e.dma_start(
            out=bvrow[0:1, :], in_=bv.rearrange("(o s) -> o s", o=1)))
        P.phase()
        P.need("dB1", P.dma_count("dB1"))
        P.need("dB1g", P.dma_count("dB1g"))
        P.op("vector", lambda e: e.tensor_scalar_mul(
            brow[0:32, 0:128], brow[0:32, 0:128], 0.125))
        P.phase()
        P.op("tensor", lambda e: e.transpose(
            self.pb[0][:, 0:32], brow[0:32, 0:128], self.ident_f32[0:32, 0:32]))
        P.op("tensor", lambda e: e.transpose(
            self.pb[1][:, 0:4], brow[0:4, 128:256], self.ident_f32[0:4, 0:4]))
        P.op("tensor", lambda e: e.matmul(
            self.pb[2][:, 0:64], self.ones_f32, brow[0:1, 256:320],
            start=True, stop=True))
        for jj in range(2):
            P.op("tensor", lambda e, jj=jj: e.matmul(
                self.pb[3 + jj][:, 0:512], self.ones_f32,
                brow[0:1, 320 + jj * 512:320 + (jj + 1) * 512],
                start=True, stop=True))
        P.phase()
        P.op("vector", lambda e: e.tensor_copy(bq8, self.pb[0][:, 0:32]))
        P.op("vector", lambda e: e.tensor_copy(bkc, self.pb[1][:, 0:4]))
        P.op("scalar", lambda e: e.copy(sinkbc, self.pb[2][:, 0:64]))
        for jj in range(2):
            P.op("scalar", lambda e, jj=jj: e.copy(
                bobc[:, jj * 512:(jj + 1) * 512], self.pb[3 + jj][:, 0:512]))
        P.phase()
        for u in range(8):
            P.op("vector", lambda e, u=u: e.tensor_scalar_mul(
                negsinkrep[:, :, u],
                sinkbc.rearrange("p (j h) -> p h j", h=2)[:, u % 2, :], -1.0))
        P.phase()
        self.abf.pop()
        k_fm = self.abf.take3(4, T)
        v_tok = self.abf.take3(TT, 512)
        self.abf.push()
        wkT = self.abf.take3(8, 512)
        wvT = self.abf.take3(8, 512)
        P.need("dW1c", cast_tgt)
        tk = self.read_wt_stripes(scr["k"], wkT, D, 512)
        tv = self.read_wt_stripes(scr["v"], wvT, D, 512)
        P.phase()
        P.need("dT", tv)
        groups = []
        for pr in range(4):
            for tq in range(4):
                def mkk(e, p, pr=pr, tq=tq):
                    inst = None
                    for ic in range(FC):
                        inst = e.matmul(
                            p, wkT[:, ic, pr * 128:(pr + 1) * 128],
                            self.xn_chunk(ic)[:, tq * 512:(tq + 1) * 512],
                            start=(ic == 0), stop=(ic == FC - 1))
                    return inst
                def evk(e, p, pr=pr, tq=tq):
                    sl = slice(tq * 512, (tq + 1) * 512)
                    return e.tensor_scalar(k_fm[:, pr, sl], p,
                                           bkc[:, pr:pr + 1], None, ALU.add)
                groups.append((mkk, evk))
        self.mm_waves(groups, evac_engines=("vector",))
        groups = []
        for t in range(TT):
            def mkv(e, p, t=t):
                for ic in range(FC):
                    e.matmul(p, self.xn_chunk(ic)[:, t * 128:(t + 1) * 128],
                             wvT[:, ic, :], start=(ic == 0), stop=False)
                return e.matmul(p, self.ones_bf[0:1, :], bvrow[0:1, :],
                                start=False, stop=True)
            def evv(e, p, t=t):
                return self._evac(e, v_tok[:, t, :], p)
            groups.append((mkv, evv))
        self.mm_waves(groups)
        self.abf.pop()
        O_q = self.abf.take3(32, 512)
        wqbuf = [self.abf.take3(8, 128), self.abf.take3(8, 128)]
        q_dup = self.abf.take3(2, 512)
        Pslab = [self.abf.take3(8, 256), self.abf.take3(8, 256)]
        PTslab = self.abf.take3(8, 256)
        wobuf = [self.abf.take(512) for _ in range(4)]
        st = self.stat
        NJ = 32

        def issue_wq(j, w):
            return P.dma("sync", f"dWq{w}", lambda e, j=j, w=w: e.dma_start(
                out=wqbuf[w],
                in_=scr["q"].rearrange("(c p) n -> p c n", p=128)[
                    :, :, j * 128:(j + 1) * 128]))

        for quar in range(4):
            wqt = {0: issue_wq(0, 0)}
            P.phase()
            for itj in range(NJ + 4):
                jA, jB, jC, jD = itj, itj - 1, itj - 2, itj - 3
                if jA < NJ:
                    P.need(f"dWq{jA % 2}", wqt[jA])
                    def mkq(e, jA=jA, quar=quar):
                        inst = None
                        for ic in range(FC):
                            inst = e.matmul(
                                self.pb[5], wqbuf[jA % 2][:, ic, :],
                                self.xn_chunk(ic)[:, quar * 512:(quar + 1) * 512],
                                start=(ic == 0), stop=(ic == FC - 1))
                        return inst
                    P.op("tensor", mkq)
                    if jA + 1 < NJ:
                        wqt[jA + 1] = issue_wq(jA + 1, (jA + 1) % 2)
                if 0 <= jB < NJ:
                    P.op("vector", lambda e, jB=jB: e.tensor_tensor(
                        st[:, 8:16], st[:, 0:8], negsinkrep[:, jB, :],
                        ALU.min))
                if 0 <= jC < NJ:
                    sp = 16 + (jC % 2) * 16
                    P.op("vector", lambda e, sp=sp: e.tensor_add(
                        st[:, sp:sp + 8], st[:, sp:sp + 8],
                        st[:, sp + 8:sp + 16]))
                if 0 <= jD < NJ:
                    for u in range(8):
                        P.op("vector", lambda e, u=u: e.tensor_copy(
                            PTslab[:, u, 0:128],
                            self.pt[0][:, u * 128:(u + 1) * 128]))
                        P.op("scalar", lambda e, u=u: e.copy(
                            PTslab[:, u, 128:256],
                            self.pt[1][:, u * 128:(u + 1) * 128]))
                P.phase()
                if jA < NJ:
                    for h2 in range(2):
                        for half in range(2):
                            P.op("vector", lambda e, jA=jA, h2=h2, half=half:
                                 e.tensor_scalar(
                                     q_dup[half * 64:half * 64 + 64, h2, :],
                                     self.pb[5][h2 * 64:h2 * 64 + 64, :],
                                     0.125, bq8[h2 * 64:h2 * 64 + 64,
                                                jA:jA + 1],
                                     ALU.mult, ALU.add))
                if 0 <= jB < NJ:
                    sp = 16 + (jB % 2) * 16
                    for u in range(8):
                        P.op("scalar", lambda e, u=u, jB=jB, sp=sp:
                             e.activation(
                                 Pslab[jB % 2][:, u, :],
                                 self.pbig.rearrange(
                                     "p (u n) -> p u n", u=8)[:, u, :],
                                 AF.Exp, bias=st[:, 8 + u:9 + u],
                                 accum_out=st[:, sp + u:sp + u + 1]))
                if 0 <= jC < NJ:
                    sp = 16 + (jC % 2) * 16
                    P.op("vector", lambda e, sp=sp: e.reciprocal(
                        st[:, sp:sp + 8], st[:, sp:sp + 8]))
                if 0 <= jD < NJ:
                    g = jD // 4
                    for u in range(8):
                        qtl, h2 = u // 2, u % 2
                        qt = quar * 4 + qtl
                        def mkpv(e, u=u, qtl=qtl, h2=h2, qt=qt, g=g):
                            inst = None
                            nt_ = 1 if qt == 0 else 2
                            for i in range(nt_):
                                ktv = qt if nt_ == 1 else qt - 1 + i
                                inst = e.matmul(
                                    self.pb[4][h2 * 64:h2 * 64 + 64,
                                               qtl * 128:(qtl + 1) * 128],
                                    v_tok[:, ktv, g * 64:(g + 1) * 64],
                                    PTslab[:, u, i * 128:(i + 1) * 128],
                                    start=(i == 0), stop=(i == nt_ - 1))
                            return inst
                        P.op("tensor", mkpv)
                P.phase()
                if jA < NJ:
                    g = jA // 4
                    kp = (g % 2) * 64
                    pr = g // 2
                    for qtl in range(4):
                        qt = quar * 4 + qtl
                        for h2 in range(2):
                            def mks(e, qtl=qtl, qt=qt, h2=h2, pr=pr, kp=kp):
                                reg = self.pb[qtl][:, h2 * 256:(h2 + 1) * 256]
                                if qt == 0:
                                    e.matmul(reg[:, 128:256], self.ident_bf,
                                             self.mask256[:, 128:256],
                                             start=True, stop=True)
                                    e.matmul(reg[:, 0:128], self.ident_bf,
                                             self.mask256[:, 0:128],
                                             start=True, stop=False)
                                    return e.matmul(
                                        reg[:, 0:128],
                                        q_dup[kp:kp + 64, h2,
                                              qtl * 128:(qtl + 1) * 128],
                                        k_fm[kp:kp + 64, pr, 0:128],
                                        start=False, stop=True)
                                e.matmul(reg, self.ident_bf, self.mask_w,
                                         start=True, stop=False)
                                return e.matmul(
                                    reg,
                                    q_dup[kp:kp + 64, h2,
                                          qtl * 128:(qtl + 1) * 128],
                                    k_fm[kp:kp + 64, pr,
                                         (qt - 1) * 128:(qt + 1) * 128],
                                    start=False, stop=True)
                            P.op("tensor", mks)
                if 0 <= jB < NJ:
                    P.op("vector", lambda e, jB=jB: e.tensor_tensor(
                        st[:, 48:56], st[:, 8:16], negsinkrep[:, jB, :],
                        ALU.subtract))
                if 0 <= jC < NJ:
                    sp = 16 + (jC % 2) * 16
                    for u in range(8):
                        P.op("vector", lambda e, u=u, jC=jC, sp=sp:
                             e.tensor_scalar(
                                 Pslab[jC % 2][:, u, :], Pslab[jC % 2][:, u, :],
                                 st[:, sp + u:sp + u + 1], None, ALU.mult))
                if 0 <= jD < NJ:
                    P.op("vector", lambda e, jD=jD: e.tensor_copy(
                        O_q[:, jD, :], self.pb[4][:, 0:512]))
                P.phase()
                if jA < NJ:
                    P.op("vector", lambda e: e.reduce_max(
                        st[:, 0:8],
                        self.pbig.rearrange("p (u n) -> p u n", u=8),
                        axis=AX.X, negate=True))
                if 0 <= jB < NJ:
                    sp = 16 + (jB % 2) * 16
                    P.op("scalar", lambda e, sp=sp: e.activation(
                        st[:, sp + 8:sp + 16], st[:, 48:56], AF.Exp))
                if 0 <= jC < NJ:
                    for u in range(8):
                        for kt in range(2):
                            P.op("tensor", lambda e, u=u, kt=kt, jC=jC:
                                 e.transpose(
                                     self.pt[kt][:, u * 128:(u + 1) * 128],
                                     Pslab[jC % 2][:, u,
                                                   kt * 128:(kt + 1) * 128],
                                     self.ident_bf))
                P.phase()
            for dh in range(2):
                def issue_wo1(j, dh=dh):
                    w = j % 4
                    return P.dma("sync", f"dWo1{w}", lambda e, j=j, w=w, dh=dh:
                                 e.dma_start(
                        out=wobuf[w], in_=scr["o"][j * 128:(j + 1) * 128,
                                                   dh * 512:(dh + 1) * 512]))
                wot = {0: issue_wo1(0), 1: issue_wo1(1)}
                P.phase()
                for j0 in range(0, 32, 2):
                    P.need(f"dWo1{j0 % 4}", wot[j0])
                    P.need(f"dWo1{(j0 + 1) % 4}", wot[j0 + 1])
                    def mko(e, j0=j0):
                        inst = None
                        for j in (j0, j0 + 1):
                            for tl in range(4):
                                inst = e.matmul(
                                    self.pb[tl],
                                    O_q[:, j, tl * 128:(tl + 1) * 128],
                                    wobuf[j % 4], start=(j == 0),
                                    stop=(j == 31))
                        return inst
                    P.op("tensor", mko)
                    for nx in (j0 + 2, j0 + 3):
                        if nx < 32:
                            wot[nx] = issue_wo1(nx)
                    P.phase()
                for tl in range(4):
                    P.op("vector", lambda e, tl=tl, dh=dh: e.tensor_tensor(
                        self.pb[tl], self.pb[tl],
                        bobc[:, dh * 512:(dh + 1) * 512], ALU.add))
                P.phase()
                for tl in range(4):
                    t = quar * 4 + tl
                    xs = self.X[:, t * D + dh * 512: t * D + (dh + 1) * 512]
                    P.op("vector", lambda e, xs=xs, tl=tl: e.tensor_tensor(
                        xs, xs, self.pb[tl], ALU.add))
                P.phase()
        self.af32.pop()
        self.abf.pop()


    def sl6_prep(self, sim_m, gates, wq, wk, wv, wo, scr):
        P = self.P
        simn = self.af32.take(128).rearrange("p (a b) -> p a b", a=8)
        sgbc = self.af32.take(16)
        for nm in ("q", "k", "v", "o"):
            src = {"q": wq, "k": wk, "v": wv, "o": wo}[nm]
            P.dma("gpsimd", "dW6", lambda e, s=src, d=scr[nm]: e.dma_start(
                out=d, in_=s))
        self.cast6_tgt = P.dma_count("dW6")
        self.abf.push()
        sq = self.abf.take(256).bitcast(F32)
        row = self.abf.take(64).bitcast(F32)
        tg = P.dma("sync", "dS6", lambda e: e.dma_start(
            out=simn, in_=sim_m.rearrange("(c p) e -> p c e", p=128)))
        tg = P.dma("sync", "dS6", lambda e: e.dma_start(
            out=row[0:1, 16:32], in_=gates.rearrange("(o e) -> o e", o=1)))
        P.phase()
        P.need("dS6", tg)
        P.need("dC", self.const_tgt)
        P.op("scalar", lambda e: e.activation(
            sq, simn.rearrange("p a b -> p (a b)"), AF.Square))
        P.phase()
        P.op("tensor", lambda e: e.matmul(
            self.pb[0][0:1, 0:128], self.onec_f32, sq, start=True, stop=True))
        P.phase()
        P.op("vector", lambda e: e.reduce_sum(
            row[0:1, 0:16],
            self.pb[0][0:1, 0:128].rearrange("p (c e) -> p e c", c=8), axis=AX.X))
        P.phase()
        P.op("vector", lambda e: e.tensor_scalar_add(row[0:1, 0:16],
                                                     row[0:1, 0:16], 1e-12))
        P.phase()
        P.op("scalar", lambda e: e.activation(row[0:1, 0:16], row[0:1, 0:16],
                                              AF.Sqrt))
        P.phase()
        P.op("vector", lambda e: e.reciprocal(row[0:1, 0:16], row[0:1, 0:16]))
        P.op("scalar", lambda e: e.activation(row[0:1, 16:32], row[0:1, 16:32],
                                              AF.Sigmoid))
        P.phase()
        P.op("tensor", lambda e: e.matmul(
            self.pb[0][:, 0:16], self.ones_f32, row[0:1, 0:16],
            start=True, stop=True))
        P.op("tensor", lambda e: e.matmul(
            self.pb[1][:, 0:16], self.ones_f32, row[0:1, 16:32],
            start=True, stop=True))
        P.phase()
        P.op("vector", lambda e: e.tensor_copy(sgbc, self.pb[1][:, 0:16]))
        P.op("scalar", lambda e: e.copy(sq[:, 0:16], self.pb[0][:, 0:16]))
        P.phase()
        for c in range(8):
            P.op("vector", lambda e, c=c: e.tensor_mul(
                simn[:, c, :], simn[:, c, :], sq[:, 0:16]))
        P.phase()
        self.abf.pop()
        return simn, sgbc

    def sl6(self, simn, sgbc, scr, first=False):
        P = self.P
        needs = [("dX", self.x_tgt), ("dC", self.const_tgt)] if first else []
        self.emit_ln(needs=needs)
        st = self.stat
        var = st[:, 48:64]
        self.abf.push()
        self.af32.push()
        rl2 = self.af32.take(16)
        rw = self.af32.take(256).rearrange("p (a b) -> p a b", a=16)
        lg = self.af32.take(16)
        cols = self.af32.take(8)
        xnT = self.abf.take(2048).bitcast(F32).rearrange(
            "p (a b) -> p a b", a=8)
        tmp32 = self.abf.take(2048).bitcast(F32)
        ebuf = self.af32.take(16)
        gat = self.af32.take(16)
        msk = self.af32.take(16)
        fb = self.af32.take(16)
        P.op("vector", lambda e: e.reciprocal(st[:, 0:16], var))
        P.phase()
        P.op("vector", lambda e: e.tensor_scalar(
            st[:, 0:16], st[:, 0:16], -1024.0 * LN_EPS, 1024.0 + 1e-12,
            ALU.mult, ALU.add))
        P.phase()
        P.op("scalar", lambda e: e.activation(st[:, 0:16], st[:, 0:16], AF.Sqrt))
        P.phase()
        P.op("vector", lambda e: e.reciprocal(rl2, st[:, 0:16]))
        P.phase()
        for t in range(TT):
            P.op("vector", lambda e, t=t: e.tensor_scalar(
                tmp32, self.X[:, t * D:(t + 1) * D],
                self.ln_b[:, t:t + 1], self.ln_rstd[:, t:t + 1],
                ALU.add, ALU.mult))
            P.phase()
            items = [(tmp32[:, fc * 128:(fc + 1) * 128], xnT[:, fc, :])
                     for fc in range(FC)]
            self.transpose_waves(items, f32=True)
            def mklg(e, t=t):
                inst = None
                for ic in range(FC):
                    inst = e.matmul(self.pb[0][:, 0:16], xnT[:, ic, :],
                                    simn[:, ic, :], start=(ic == 0),
                                    stop=(ic == FC - 1))
                return inst
            P.op("tensor", mklg)
            P.phase()
            P.op("vector", lambda e, t=t: e.scalar_tensor_tensor(
                lg, self.pb[0][:, 0:16], rl2[:, t:t + 1], sgbc,
                ALU.mult, ALU.subtract))
            P.phase()
            P.op("vector", lambda e: e.reduce_max(cols[:, 0:1], lg, axis=AX.X))
            P.op("vector", lambda e: e.tensor_scalar_max(gat, lg, 0.0))
            P.phase()
            P.op("vector", lambda e: e.tensor_scalar(
                ebuf, lg, cols[:, 0:1], None, ALU.is_ge))
            P.op("vector", lambda e: e.tensor_single_scalar(
                cols[:, 2:3], cols[:, 0:1], 0.0, ALU.is_le))
            P.op("scalar", lambda e: e.activation(cols[:, 3:4], cols[:, 0:1],
                                                  AF.Relu))
            P.phase()
            P.op("vector", lambda e: e.scalar_tensor_tensor(
                ebuf, ebuf, -1e30, lg, ALU.mult, ALU.add))
            P.op("vector", lambda e: e.tensor_scalar_mul(cols[:, 4:5],
                                                         cols[:, 3:4], -1.0))
            P.phase()
            P.op("vector", lambda e: e.reduce_max(cols[:, 1:2], ebuf, axis=AX.X))
            P.phase()
            P.op("vector", lambda e: e.tensor_scalar(
                fb, lg, cols[:, 1:2], None, ALU.is_ge))
            P.op("vector", lambda e: e.tensor_single_scalar(msk, lg, 0.0,
                                                            ALU.is_gt))
            P.phase()
            P.op("vector", lambda e: e.scalar_tensor_tensor(
                msk, fb, cols[:, 2:3], msk, ALU.mult, ALU.add))
            P.phase()
            P.op("vector", lambda e: e.tensor_single_scalar(msk, msk, 0.0,
                                                            ALU.is_gt))
            P.phase()
            P.op("scalar", lambda e: e.activation(ebuf, gat, AF.Exp,
                                                  bias=cols[:, 4:5]))
            P.phase()
            P.op("vector", lambda e: e.tensor_mul(ebuf, ebuf, msk))
            P.phase()
            P.op("vector", lambda e: e.reduce_sum(cols[:, 5:6], ebuf, axis=AX.X))
            P.phase()
            P.op("vector", lambda e: e.reciprocal(cols[:, 5:6], cols[:, 5:6]))
            P.phase()
            P.op("vector", lambda e, t=t: e.tensor_scalar(
                rw[:, t, :], ebuf, cols[:, 5:6], None, ALU.mult))
            P.phase()
        wbuf = [self.abf.take3(8, 128), self.abf.take3(8, 128)]
        acc32 = self.abf.take(1024).bitcast(F32).rearrange(
            "p (a b) -> p a b", a=4)
        q_tok = self.abf.take3(TT, 128)
        k_tok = self.abf.take3(TT, 128)
        v_tok = self.abf.take3(TT, 128)
        q_fm = self.abf.take(T)
        k_fm = self.abf.take(T)
        ao_fm = self.abf.take3(TT, 128)
        P_row = self.abf.take(T)
        P_T = self.abf.take3(TT, 128)
        wobuf = [self.abf.take(512), self.abf.take(512)]
        P.phase()
        P.need("dW6", self.cast6_tgt)
        for proj, dst in (("q", q_tok), ("k", k_tok), ("v", v_tok)):
            s = scr[proj]
            def issue_w(e_, w, s=s):
                return P.dma("sync", f"dW6{w}", lambda e, e_=e_, w=w, s=s: e.dma_start(
                    out=wbuf[w], in_=s.rearrange("(E c p) n -> E p c n", E=16, p=128)[e_]))
            for quar in range(4):
                wtgt = {0: issue_w(0, 0)}
                P.phase()
                for e_ in range(17):
                    if e_ < 16:
                        w = e_ % 2
                        P.need(f"dW6{w}", wtgt[e_])
                        pbk = 4 + (e_ % 2)
                        def mkp(e, quar=quar, w=w, pbk=pbk):
                            inst = None
                            for tl in range(4):
                                t = quar * 4 + tl
                                for ic in range(FC):
                                    inst = e.matmul(
                                        self.pb[pbk][:, tl * 128:(tl + 1) * 128],
                                        self.xn_chunk(ic)[:, t * 128:(t + 1) * 128],
                                        wbuf[w][:, ic, :], start=(ic == 0),
                                        stop=(ic == FC - 1))
                            return inst
                        P.op("tensor", mkp)
                        if e_ + 1 < 16:
                            wtgt[e_ + 1] = issue_w(e_ + 1, (e_ + 1) % 2)
                    if e_ >= 1:
                        em = e_ - 1
                        pbp = 4 + (em % 2)
                        for tl in range(4):
                            t = quar * 4 + tl
                            if em == 0:
                                P.op("vector", lambda e, t=t, tl=tl, pbp=pbp:
                                     e.tensor_scalar(
                                         acc32[:, tl, :],
                                         self.pb[pbp][:, tl * 128:(tl + 1) * 128],
                                         rw[:, t, 0:1], None, ALU.mult))
                            else:
                                P.op("vector", lambda e, t=t, tl=tl, em=em, pbp=pbp:
                                     e.scalar_tensor_tensor(
                                         acc32[:, tl, :],
                                         self.pb[pbp][:, tl * 128:(tl + 1) * 128],
                                         rw[:, t, em:em + 1], acc32[:, tl, :],
                                         ALU.mult, ALU.add))
                    P.phase()
                csc = 1.0 / np.sqrt(128.0) if proj == "q" else 1.0
                for tl in range(4):
                    t = quar * 4 + tl
                    P.op("scalar", lambda e, t=t, tl=tl, dst=dst, csc=csc:
                         e.activation(dst[:, t, :], acc32[:, tl, :], AF.Copy,
                                      scale=csc))
                P.phase()
        items = [(q_tok[:, t, :], q_fm[:, t * 128:(t + 1) * 128]) for t in range(TT)]
        items += [(k_tok[:, t, :], k_fm[:, t * 128:(t + 1) * 128]) for t in range(TT)]
        self.transpose_waves(items)
        sc = 1.0 / np.sqrt(128.0)
        for qt in range(TT):
            ksp = (qt + 1) * 128
            nb = (ksp + 511) // 512
            dkb = qt // 4
            doff = (qt % 4) * 128
            for kb in range(nb):
                N = min(512, ksp - kb * 512)
                def mks(e, kb=kb, N=N, qt=qt, doff=doff, diag_=(kb == dkb)):
                    if diag_:
                        e.matmul(self.pb[kb][:, 0:N], self.ident_bf,
                                 self.maskpad[:, 384 - doff:384 - doff + N],
                                 start=True, stop=False)
                    return e.matmul(
                        self.pb[kb][:, 0:N], q_fm[:, qt * 128:(qt + 1) * 128],
                        k_fm[:, kb * 512:kb * 512 + N],
                        start=(not diag_), stop=True)
                P.op("tensor", mks)
            P.phase()
            P.op("vector", lambda e, ksp=ksp: e.reduce_max(
                st[:, 9:10], self.pbig[:, 0:ksp], axis=AX.X, negate=True))
            P.phase()
            P.op("scalar", lambda e, ksp=ksp: e.activation(
                P_row[:, 0:ksp], self.pbig[:, 0:ksp], AF.Exp,
                bias=st[:, 9:10], scale=1.0, accum_out=st[:, 24:25]))
            P.phase()
            P.op("vector", lambda e: e.reciprocal(st[:, 25:26], st[:, 24:25]))
            P.phase()
            P.op("vector", lambda e, ksp=ksp: e.tensor_scalar(
                P_row[:, 0:ksp], P_row[:, 0:ksp], st[:, 25:26], None, ALU.mult))
            P.phase()
            items = [(P_row[:, kt * 128:(kt + 1) * 128], P_T[:, kt, :])
                     for kt in range(qt + 1)]
            self.transpose_waves(items)
            def mkpv(e, qt=qt):
                inst = None
                for kt in range(qt + 1):
                    inst = e.matmul(self.pb[4][:, 0:128], v_tok[:, kt, :],
                                    P_T[:, kt, :], start=(kt == 0),
                                    stop=(kt == qt))
                return inst
            P.op("tensor", mkpv)
            P.phase()
            P.op("vector", lambda e, qt=qt: e.tensor_copy(
                ao_fm[:, qt, :], self.pb[4][:, 0:128]))
            P.phase()
        def issue_wo(e_, ch, w):
            return P.dma("sync", f"dWo{w}", lambda e, e_=e_, ch=ch, w=w: e.dma_start(
                out=wobuf[w],
                in_=scr["o"][e_ * 128:(e_ + 1) * 128, ch * 512:(ch + 1) * 512]))
        for quar in range(4):
            for ch in range(2):
                wtgt = {0: issue_wo(0, ch, 0)}
                P.phase()
                for st_ in range(33):
                    if st_ < 32:
                        e_, half = st_ // 2, st_ % 2
                        if half == 0:
                            P.need(f"dWo{e_ % 2}", wtgt[e_])
                        b0 = 0 if st_ % 2 == 0 else 2
                        def mkoe(e, quar=quar, st_=st_, b0=b0):
                            e_, half = st_ // 2, st_ % 2
                            inst = None
                            for i in range(2):
                                t = quar * 4 + half * 2 + i
                                inst = e.matmul(self.pb[b0 + i], ao_fm[:, t, :],
                                                wobuf[e_ % 2], start=True,
                                                stop=True)
                            return inst
                        P.op("tensor", mkoe)
                        if half == 1 and e_ + 1 < 16:
                            wtgt[e_ + 1] = issue_wo(e_ + 1, ch, (e_ + 1) % 2)
                    if st_ >= 1:
                        sm = st_ - 1
                        em, hm = sm // 2, sm % 2
                        b0 = 0 if sm % 2 == 0 else 2
                        for i in range(2):
                            t = quar * 4 + hm * 2 + i
                            xs = self.X[:, t * D + ch * 512:
                                        t * D + (ch + 1) * 512]
                            P.op("vector", lambda e, xs=xs, i=i, b0=b0, em=em,
                                 t=t: e.scalar_tensor_tensor(
                                     xs, self.pb[b0 + i], rw[:, t, em:em + 1],
                                     xs, ALU.mult, ALU.add))
                    P.phase()
        self.af32.pop()
        self.abf.pop()


    def _elu1_inplace(self, units, slots):
        P = self.P
        nu = len(units)
        for i in range(nu + 2):
            if i < nu:
                x = units[i]
                s = slots[i % 3]
                n = x.shape[-1]
                P.op("vector", lambda e, x=x, s=s, n=n: e.tensor_scalar_min(
                    s[:, 0:n], x, 0.0))
            if 0 <= i - 1 < nu:
                x = units[i - 1]
                s = slots[(i - 1) % 3]
                n = x.shape[-1]
                P.op("scalar", lambda e, s=s, n=n: e.activation(
                    s[:, 0:n], s[:, 0:n], AF.Exp))
            if 0 <= i - 2 < nu:
                x = units[i - 2]
                s = slots[(i - 2) % 3]
                n = x.shape[-1]
                P.op("vector", lambda e, x=x, s=s, n=n: e.scalar_tensor_tensor(
                    x, x, 0.0, s[:, 0:n], ALU.max, ALU.add))
            P.phase()

    def sl5(self, wqkv, wout, gamma, beta, scr_qkv, scr_out, first=False, upto="full"):
        P = self.P
        P.dma("gpsimd", "dW5", lambda e: e.dma_start(out=scr_qkv, in_=wqkv))
        P.dma("gpsimd", "dW5", lambda e: e.dma_start(out=scr_out, in_=wout))
        cast_tgt = P.dma_count("dW5")
        P.phase()
        needs = [("dX", self.x_tgt), ("dC", self.const_tgt)] if first else []
        self.emit_ln(needs=needs, upto="lncol")
        st = self.stat
        var = st[:, 48:64]
        s2 = self.af32.take(16)
        P.op("vector", lambda e: e.tensor_mul(st[:, 0:16], self.ln_rstd,
                                              self.ln_rstd))
        P.phase()
        P.op("vector", lambda e: e.scalar_tensor_tensor(
            st[:, 0:16], st[:, 0:16], 1.0, var, ALU.mult, ALU.mult))
        P.phase()
        P.op("vector", lambda e: e.tensor_scalar_add(st[:, 0:16], st[:, 0:16],
                                                     LN_EPS))
        P.phase()
        P.op("scalar", lambda e: e.activation(st[:, 16:32], st[:, 0:16], AF.Sqrt))
        P.phase()
        P.op("vector", lambda e: e.reciprocal(st[:, 16:32], st[:, 16:32]))
        P.phase()
        P.op("vector", lambda e: e.tensor_mul(s2, self.ln_rstd, st[:, 16:32]))
        P.phase()
        self.abf.push()
        gbr = self.abf.take(4096).bitcast(F32)
        gbc = self.abf.take(8192).bitcast(F32)
        tmp2 = self.abf.take(4096).bitcast(F32)
        tgb = P.dma("sync", "dG5", lambda e: e.dma_start(
            out=gbr[0:1, 0:1024], in_=gamma.rearrange("(o d) -> o d", o=1)))
        tgb = P.dma("sync", "dG5", lambda e: e.dma_start(
            out=gbr[0:1, 1024:2048], in_=beta.rearrange("(o d) -> o d", o=1)))
        P.phase()
        P.need("dG5", tgb)
        for j in range(4):
            P.op("tensor", lambda e, j=j: e.matmul(
                self.pb[j][:, 0:512], self.ones_f32[0:1, :],
                gbr[0:1, j * 512:(j + 1) * 512], start=True, stop=True))
        P.phase()
        for j in range(4):
            P.op("vector" if j % 2 == 0 else "scalar",
                 lambda e, j=j: e.tensor_copy(gbc[:, j * 512:(j + 1) * 512],
                                              self.pb[j][:, 0:512])
                 if hasattr(e, "tensor_copy") else e.copy(
                     gbc[:, j * 512:(j + 1) * 512], self.pb[j][:, 0:512]))
        P.phase()
        gamma_bc = gbc[:, 0:1024]
        beta_bc = gbc[:, 1024:2048]
        for g in range(TT // 2):
            for i in range(2):
                t = 2 * g + i
                P.op("vector",
                     lambda e, t=t, i=i: e.tensor_scalar(
                         tmp2[:, i * 1024:(i + 1) * 1024],
                         self.X[:, t * D:(t + 1) * D],
                         self.ln_b[:, t:t + 1], s2[:, t:t + 1],
                         ALU.add, ALU.mult))
            P.phase()
            for i in range(2):
                P.op("vector",
                     lambda e, i=i: e.tensor_mul(
                         tmp2[:, i * 1024:(i + 1) * 1024],
                         tmp2[:, i * 1024:(i + 1) * 1024], gamma_bc))
            P.phase()
            for i in range(2):
                P.op("vector",
                     lambda e, i=i: e.tensor_add(
                         self.xtbuf[:, i, :],
                         tmp2[:, i * 1024:(i + 1) * 1024], beta_bc))
            P.phase()
            items = []
            for i in range(2):
                t = 2 * g + i
                for fc in range(FC):
                    items.append((self.xtbuf[:, i, fc * 128:(fc + 1) * 128],
                                  self.xn_fm[:, fc * T + t * 128:fc * T + (t + 1) * 128]))
            self.transpose_waves(items)
        self.abf.pop()
        if upto == "xn2":
            return
        self.abf.push()
        wbuf = self.abf.take3(8, 512)
        R1 = self.abf.take3(TT, D)
        R2 = self.abf.take3(TT, D)
        kv_sb = self.abf.take3(16, 64)
        ksum_bf = self.abf.take(16)
        elu3 = self.abf.take(1024)
        dcol = self.af32.take(16)
        q_fm = R1.rearrange("p a b -> p (a b)").rearrange(
            "p (a b) -> p a b", a=FC)
        P.phase()
        P.need("dW5", cast_tgt)
        for blk, dstR, kind in (("k", R1, "tok"), ("v", R2, "tok")):
            off = 1024 if blk == "k" else 2048
            for oh in range(2):
                tw = P.dma("sync", "dWB5", lambda e, off=off, oh=oh: e.dma_start(
                    out=wbuf, in_=scr_qkv.rearrange("(c p) n -> p c n", p=128)[
                        :, :, off + oh * 512: off + (oh + 1) * 512]))
                P.phase()
                P.need("dWB5", tw)
                groups = []
                for t in range(TT):
                    def mkp(e, p, t=t, oh=oh):
                        inst = None
                        for ic in range(FC):
                            inst = e.matmul(
                                p, self.xn_chunk(ic)[:, t * 128:(t + 1) * 128],
                                wbuf[:, ic, :], start=(ic == 0),
                                stop=(ic == FC - 1))
                        return inst
                    def evp(e, p, t=t, oh=oh, dstR=dstR):
                        return self._evac(e, dstR[:, t, oh * 512:(oh + 1) * 512], p)
                    groups.append((mkp, evp))
                self.mm_waves(groups)
        if upto == "kv":
            return
        self._elu1_inplace([R1[:, t, :] for t in range(TT)],
                           [self.xtbuf[:, 0, :], self.xtbuf[:, 1, :], elu3])
        if upto == "eluk":
            return
        for h in range(16):
            def mkkv(e, h=h):
                inst = None
                for kt in range(TT):
                    inst = e.matmul(self.pb[0][0:64, 0:64],
                                    R1[:, kt, h * 64:(h + 1) * 64],
                                    R2[:, kt, h * 64:(h + 1) * 64],
                                    start=(kt == 0), stop=(kt == TT - 1))
                return inst
            def mkks(e, h=h):
                inst = None
                for kt in range(TT):
                    inst = e.matmul(self.pb[1][0:64, 0:1],
                                    R1[:, kt, h * 64:(h + 1) * 64],
                                    self.ones_bf[:, 0:1],
                                    start=(kt == 0), stop=(kt == TT - 1))
                return inst
            P.op("tensor", mkkv)
            P.op("tensor", mkks)
            P.phase()
            P.op("vector", lambda e, h=h: e.tensor_copy(
                kv_sb[0:64, h, :], self.pb[0][0:64, 0:64]))
            P.op("vector", lambda e, h=h: e.tensor_copy(
                kv_sb[64:128, h, :], self.pb[0][0:64, 0:64]))
            P.op("scalar", lambda e, h=h: e.copy(
                ksum_bf[0:64, h:h + 1], self.pb[1][0:64, 0:1]))
            P.op("scalar", lambda e, h=h: e.copy(
                ksum_bf[64:128, h:h + 1], self.pb[1][0:64, 0:1]))
            P.phase()
        if upto == "kvsum":
            return
        for oh in range(2):
            tw = P.dma("sync", "dWB5", lambda e, oh=oh: e.dma_start(
                out=wbuf, in_=scr_qkv.rearrange("(c p) n -> p c n", p=128)[
                    :, :, oh * 512:(oh + 1) * 512]))
            P.phase()
            P.need("dWB5", tw)
            groups = []
            for ocl in range(4):
                oc = oh * 4 + ocl
                for tq in range(4):
                    def mkq(e, p, ocl=ocl, tq=tq):
                        inst = None
                        for ic in range(FC):
                            inst = e.matmul(
                                p, wbuf[:, ic, ocl * 128:(ocl + 1) * 128],
                                self.xn_chunk(ic)[:, tq * 512:(tq + 1) * 512],
                                start=(ic == 0), stop=(ic == FC - 1))
                        return inst
                    def evq(e, p, oc=oc, tq=tq):
                        return self._evac(e, q_fm[:, oc, tq * 512:(tq + 1) * 512], p)
                    groups.append((mkq, evq))
            self.mm_waves(groups)
        units = []
        for oc in range(FC):
            units.append(q_fm[:, oc, 0:1024])
            units.append(q_fm[:, oc, 1024:2048])
        self._elu1_inplace(units, [self.xtbuf[:, 0, :], self.xtbuf[:, 1, :], elu3])
        if upto == "qproj":
            return
        for t in range(TT):
            for h in range(16):
                hb = (h % 2) * 64
                db = 2 if h % 2 == 0 else 5
                P.op("tensor", lambda e, h=h, hb=hb, t=t, db=db: e.matmul(
                    self.pb[db][:, h // 2:h // 2 + 1],
                    q_fm[hb:hb + 64, h // 2, t * 128:(t + 1) * 128],
                    ksum_bf[hb:hb + 64, h:h + 1], start=True, stop=True))
                qb = 3 + h % 2
                P.op("tensor", lambda e, h=h, hb=hb, t=t, qb=qb: e.matmul(
                    self.pb[qb][:, (h // 2) * 64:(h // 2 + 1) * 64],
                    q_fm[hb:hb + 64, h // 2, t * 128:(t + 1) * 128],
                    kv_sb[hb:hb + 64, h, :], start=True, stop=True))
            P.phase()
            P.op("vector", lambda e: e.tensor_scalar_add(
                dcol[:, 0:8], self.pb[2][:, 0:8], 1e-6))
            P.op("vector", lambda e: e.tensor_scalar_add(
                dcol[:, 8:16], self.pb[5][:, 0:8], 1e-6))
            P.phase()
            P.op("vector", lambda e: e.reciprocal(dcol, dcol))
            P.phase()
            for h in range(16):
                qb = 3 + h % 2
                dc = (h % 2) * 8 + h // 2
                P.op("vector", lambda e, h=h, t=t, qb=qb, dc=dc: e.tensor_scalar(
                    R2[:, t, h * 64:(h + 1) * 64],
                    self.pb[qb][:, (h // 2) * 64:(h // 2 + 1) * 64],
                    dcol[:, dc:dc + 1], None, ALU.mult))
            P.phase()
        if upto == "attn":
            return
        for g in range(TT // 2):
            items = []
            for i in range(2):
                t = 2 * g + i
                for fc in range(FC):
                    items.append((R2[:, t, fc * 128:(fc + 1) * 128],
                                  self.xn_fm[:, fc * T + t * 128:fc * T + (t + 1) * 128]))
            self.transpose_waves(items)
        tmpr = self.abf.take(2048).bitcast(F32)
        for oh in range(2):
            tw = P.dma("sync", "dWB5", lambda e, oh=oh: e.dma_start(
                out=wbuf, in_=scr_out.rearrange("(c p) n -> p c n", p=128)[
                    :, :, oh * 512:(oh + 1) * 512]))
            P.phase()
            P.need("dWB5", tw)
            for t in range(TT):
                bk = t % 2
                def mkо(e, t=t, bk=bk):
                    inst = None
                    for ic in range(FC):
                        inst = e.matmul(
                            self.pb[bk], self.xn_chunk(ic)[:, t * 128:(t + 1) * 128],
                            wbuf[:, ic, :], start=(ic == 0), stop=(ic == FC - 1))
                    return inst
                P.op("tensor", mkо)
                P.op("vector", lambda e, t=t, oh=oh: e.tensor_scalar(
                    tmpr[:, (t % 2) * 512:(t % 2) * 512 + 512],
                    self.X[:, t * D + oh * 512:t * D + oh * 512 + 512],
                    self.ln_b[:, t:t + 1], self.ln_rstd[:, t:t + 1],
                    ALU.add, ALU.mult))
                P.phase()
                P.op("vector", lambda e, t=t, bk=bk: e.tensor_tensor(
                    tmpr[:, bk * 512:bk * 512 + 512],
                    tmpr[:, bk * 512:bk * 512 + 512], self.pb[bk], ALU.add))
                P.phase()
                xs = self.X[:, t * D + oh * 512:t * D + oh * 512 + 512]
                P.op("vector", lambda e, xs=xs, bk=bk: e.tensor_tensor(
                    xs, xs, tmpr[:, bk * 512:bk * 512 + 512], ALU.add))
                P.phase()
        self.abf.pop()


    def sl3(self, wq, wk, wv, wo, scr, first=False):
        P = self.P
        names = ["q", "k", "v", "o"]
        wts = [wq, wk, wv, wo]
        for i in range(4):
            cast_tgt = P.dma("gpsimd", "dW3",
                             lambda e, s=wts[i], d=scr[i]: e.dma_start(out=d, in_=s))
        P.phase()
        needs = [("dX", self.x_tgt), ("dC", self.const_tgt)] if first else []
        self.emit_ln(needs=needs)
        self.abf.push()
        wqt = self.abf.take3(8, D)
        k_fm = self.abf.take3(8, T)
        v_tok = self.abf.take3(TT, D)
        q_qt = self.abf.take3(8, 128)
        O_q = self.abf.take3(8, 512)
        pa = self.abf.take(4096)
        P_row = pa[:, 0:2048]
        P_rowB = self.xtbuf.rearrange("p a b -> p (a b)")
        P_T = pa[:, 2048:4096].rearrange("p (a b) -> p a b", a=16)
        wot = pa.rearrange("p (a b) -> p a b", a=8)
        st = self.stat

        P.need("dW3", P.dma_count("dW3"))
        tgt = self.read_wt_stripes(scr[1], wqt, D, D)
        P.phase()
        self._proj_fm(k_fm, wqt, tgt, scale=1.0)
        tgt = self.read_wt_stripes(scr[2], wqt, D, D)
        P.phase()
        self._proj_tok(v_tok, wqt, tgt)
        tgt = self.read_wt_stripes(scr[0], wqt, D, D)
        P.phase()
        P.need("dT", tgt)
        P.op("sync", lambda e: e.nop())
        P.phase()
        for quar in range(4):
            for qtl in range(4):
                qt = quar * 4 + qtl
                groups = []
                for oc in range(FC):
                    def mkq(e, p, oc=oc, qt=qt):
                        inst = None
                        for ic in range(FC):
                            inst = e.matmul(
                                p[:, 0:128], wqt[:, ic, oc * 128:(oc + 1) * 128],
                                self.xn_chunk(ic)[:, qt * 128:(qt + 1) * 128],
                                start=(ic == 0), stop=(ic == FC - 1))
                        return inst
                    def evq(e, p, oc=oc):
                        return self._evac(e, q_qt[:, oc, :], p[:, 0:128], 0.125)
                    groups.append((mkq, evq))
                self.mm_waves(groups)
                ksp = (qt + 1) * 128
                nb = (ksp + 511) // 512
                dkb = qt // 4
                doff = (qt % 4) * 128
                nt = qt + 1

                def prow(h):
                    return P_row if h % 2 == 0 else P_rowB

                for it in range(19):
                    hS, hT, hP, hO = it, it - 1, it - 2, it - 3
                    if hS < 16:
                        for kb in range(nb):
                            N = min(512, ksp - kb * 512)
                            def mks(e, kb=kb, N=N, pr=hS // 2,
                                    hb=(hS % 2) * 64, doff=doff,
                                    diag=(kb == dkb)):
                                if diag:
                                    e.matmul(self.pb[kb][:, 0:N], self.ident_bf,
                                             self.maskpad[:, 384 - doff:384 - doff + N],
                                             start=True, stop=False)
                                return e.matmul(
                                    self.pb[kb][:, 0:N],
                                    q_qt[hb:hb + 64, pr, :],
                                    k_fm[hb:hb + 64, pr, kb * 512:kb * 512 + N],
                                    start=(not diag), stop=True)
                            P.op("tensor", mks)
                    if 0 <= hP < 16 and nt > 8:
                        for kt in range(8, nt):
                            P.op("tensor", lambda e, kt=kt, hP=hP: e.transpose(
                                self.pt[1][:, (kt - 8) * 128:(kt - 7) * 128],
                                prow(hP)[:, kt * 128:(kt + 1) * 128],
                                self.ident_bf))
                    if 0 <= hT < 16:
                        P.op("vector", lambda e: e.reciprocal(st[:, 25:26],
                                                              st[:, 24:25]))
                    if 0 <= hP < 16:
                        for kt in range(min(8, nt)):
                            P.op("vector", lambda e, kt=kt: e.tensor_copy(
                                P_T[:, kt, :],
                                self.pt[0][:, kt * 128:(kt + 1) * 128]))
                    if 0 <= hO < 16 and hO % 2 == 1:
                        P.op("vector", lambda e, pr=hO // 2, qtl=qtl:
                             e.tensor_copy(
                                 O_q[:, pr, qtl * 128:(qtl + 1) * 128],
                                 self.pb[4][:, 0:128]))
                    P.phase()
                    if hS < 16:
                        P.op("vector", lambda e, ksp=ksp: e.reduce_max(
                            st[:, 9:10], self.pbig[:, 0:ksp], axis=AX.X,
                            negate=True))
                    if 0 <= hT < 16:
                        P.op("vector", lambda e, hT=hT, ksp=ksp: e.tensor_scalar(
                            prow(hT)[:, 0:ksp], prow(hT)[:, 0:ksp],
                            st[:, 25:26], None, ALU.mult))
                    if 0 <= hP < 16 and nt > 8:
                        for kt in range(8, nt):
                            P.op("scalar", lambda e, kt=kt: e.copy(
                                P_T[:, kt, :],
                                self.pt[1][:, (kt - 8) * 128:(kt - 7) * 128]))
                    P.phase()
                    if hS < 16:
                        P.op("scalar", lambda e, ksp=ksp, hS=hS: e.activation(
                            prow(hS)[:, 0:ksp], self.pbig[:, 0:ksp],
                            AF.Exp, bias=st[:, 9:10], scale=1.0,
                            accum_out=st[:, 24:25]))
                    if 0 <= hT < 16:
                        for kt in range(min(8, nt)):
                            P.op("tensor", lambda e, kt=kt, hT=hT: e.transpose(
                                self.pt[0][:, kt * 128:(kt + 1) * 128],
                                prow(hT)[:, kt * 128:(kt + 1) * 128],
                                self.ident_bf))
                    if 0 <= hP < 16:
                        def mkpv(e, qt=qt, hP=hP, hb=(hP % 2) * 64):
                            inst = None
                            for kt in range(qt + 1):
                                inst = e.matmul(
                                    self.pb[4][hb:hb + 64, 0:128],
                                    v_tok[:, kt, hP * 64:(hP + 1) * 64],
                                    P_T[:, kt, :],
                                    start=(kt == 0), stop=(kt == qt))
                            return inst
                        P.op("tensor", mkpv)
                    P.phase()
            for dh in range(2):
                tgt = P.dma("sync", "dT", lambda e, dh=dh: e.dma_start(
                    out=wot,
                    in_=scr[3].rearrange("(f p) m -> p f m", p=128)[
                        :, :, dh * 512:(dh + 1) * 512]))
                P.phase()
                P.need("dT", tgt)
                for tl in range(4):
                    def mko(e, tl=tl, dh=dh):
                        inst = None
                        for fc in range(FC):
                            inst = e.matmul(
                                self.pb[tl], O_q[:, fc, tl * 128:(tl + 1) * 128],
                                wot[:, fc, :], start=(fc == 0), stop=(fc == FC - 1))
                        return inst
                    P.op("tensor", mko)
                P.phase()
                for tl in range(4):
                    t = quar * 4 + tl
                    xs = self.X[:, t * D + dh * 512: t * D + (dh + 1) * 512]
                    P.op("vector", lambda e, xs=xs, tl=tl: e.tensor_tensor(
                        xs, xs, self.pb[tl], ALU.add))
                P.phase()
        self.abf.pop()

    def _proj_fm(self, dst, wt, dma_tgt, scale=1.0, tqs=None, toff=0):
        P = self.P
        P.need("dT", dma_tgt)
        if tqs is None:
            tqs = range(T // 512)
        groups = []
        for oc in range(FC):
            for tq in tqs:
                d0 = tq * 512 - toff
                def mk(e, p, oc=oc, tq=tq):
                    inst = None
                    for ic in range(FC):
                        inst = e.matmul(p, wt[:, ic, oc * 128:(oc + 1) * 128],
                                        self.xn_chunk(ic)[:, tq * 512:(tq + 1) * 512],
                                        start=(ic == 0), stop=(ic == FC - 1))
                    return inst
                def ev(e, p, oc=oc, d0=d0):
                    return self._evac(e, dst[:, oc, d0:d0 + 512], p, scale)
                groups.append((mk, ev))
        self.mm_waves(groups)

    def _proj_tok(self, dst, wt, dma_tgt):
        P = self.P
        P.need("dT", dma_tgt)
        groups = []
        for t in range(TT):
            for oh in range(2):
                def mk(e, p, t=t, oh=oh):
                    inst = None
                    for ic in range(FC):
                        inst = e.matmul(p, self.xn_chunk(ic)[:, t * 128:(t + 1) * 128],
                                        wt[:, ic, oh * 512:(oh + 1) * 512],
                                        start=(ic == 0), stop=(ic == FC - 1))
                    return inst
                def ev(e, p, t=t, oh=oh):
                    return self._evac(e, dst[:, t, oh * 512:(oh + 1) * 512], p)
                groups.append((mk, ev))
        self.mm_waves(groups)


def host_consts():
    qq = np.arange(128)[:, None]
    c = np.arange(256)[None, :]
    allowed = np.where(c < 128, c > qq, (c - 128) <= qq)
    mask_w = np.where(allowed, 0.0, NEG).astype(np.float32)
    maskpad = np.zeros((128, 1024), np.float32)
    maskpad[:, 384:512] = mask_w[:, 128:256]
    mask256 = np.concatenate([mask_w[:, 128:256],
                              np.full((128, 128), NEG, np.float32)], axis=1)
    return {
        "ident_bf": np.eye(128, dtype=np.float32).astype(bfdt),
        "ident_f32": np.eye(128, dtype=np.float32),
        "mask_w": mask_w.astype(bfdt),
        "mask256": mask256.astype(bfdt),
        "maskpad": maskpad.astype(bfdt),
        "ones_bf": np.ones((128, 128), np.float32).astype(bfdt),
        "ones_f32": np.ones((1, 128), np.float32),
        "onec_f32": np.ones((128, 1), np.float32),
    }


CONST_SPECS = {
    "ident_bf": ([128, 128], BF16),
    "ident_f32": ([128, 128], F32),
    "mask_w": ([128, 256], BF16),
    "mask256": ([128, 256], BF16),
    "maskpad": ([128, 1024], BF16),
    "ones_bf": ([128, 128], BF16),
    "ones_f32": ([1, 128], F32),
    "onec_f32": ([128, 1], F32),
}


def build_sl2_launch():
    nc = bass.Bass("TRN2", target_bir_lowering=False)
    es = ExitStack()
    xin = nc.dram_tensor("xin", [T, D], F32, kind="ExternalInput").ap()
    xout = nc.dram_tensor("xout", [T, D], F32, kind="ExternalOutput").ap()
    w_fc = nc.dram_tensor("w_fc", [4096, D], F32, kind="ExternalInput").ap()
    w_proj = nc.dram_tensor("w_proj", [D, 4096], F32, kind="ExternalInput").ap()
    u = nc.dram_tensor("u", [4096, 16], F32, kind="ExternalInput").ap()
    v = nc.dram_tensor("v", [16, D], F32, kind="ExternalInput").ap()
    fb = nc.dram_tensor("fb", [16, 16], F32, kind="ExternalInput").ap()
    gate = nc.dram_tensor("gate", [1, 1], F32, kind="ExternalInput").ap()
    scr_fc = nc.dram_tensor("scr_fc", [4096, D], BF16, kind="Internal").ap()
    scr_pr = nc.dram_tensor("scr_pr", [D, 4096], BF16, kind="Internal").ap()
    cmap = {}
    for name, (shape, dt) in CONST_SPECS.items():
        cmap[name] = nc.dram_tensor("c_" + name, shape, dt, kind="ExternalInput").ap()
    b = B(nc, es)
    b.load_consts(cmap)
    b.load_x(xin)
    b.sl2_prep(w_fc, w_proj, u, v, fb, gate, scr_fc, scr_pr)
    b.sl2(scr_fc, scr_pr, first=True)
    b.store_x(xout)
    b.P.replay()
    es.close()
    return nc



SEQ = ["sl1", "sl2", "sl3", "sl2b", "sl5", "sl6", "sl7", "sl6b"]


def build_launch(ids):
    nc = bass.Bass("TRN2", target_bir_lowering=False)
    es = ExitStack()
    xin = nc.dram_tensor("xin", [T, D], F32, kind="ExternalInput").ap()
    xout = nc.dram_tensor("xout", [T, D], F32, kind="ExternalOutput").ap()
    names = set(SEQ[i] for i in ids)
    dr = {}

    def din(name, shape):
        dr[name] = nc.dram_tensor(name, shape, F32, kind="ExternalInput").ap()
        return dr[name]

    def scrt(name, shape):
        return nc.dram_tensor(name, shape, BF16, kind="Internal").ap()

    if "sl1" in names:
        for nm, sh in (("L0_attn_a_Wq_T", [D, 4096]), ("L0_attn_a_bq", [4096]),
                       ("L0_attn_a_Wk_T", [D, 512]), ("L0_attn_a_bk", [512]),
                       ("L0_attn_a_Wv_T", [D, 512]), ("L0_attn_a_bv", [512]),
                       ("L0_attn_a_Wo_T", [4096, D]), ("L0_attn_a_bo", [D]),
                       ("L0_attn_a_sinks", [64])):
            din(nm, sh)
        scr1 = {"q": scrt("s1q", [D, 4096]), "k": scrt("s1k", [D, 512]),
                "v": scrt("s1v", [D, 512]), "o": scrt("s1o", [4096, D])}
    if "sl2" in names or "sl2b" in names:
        for nm, sh in (("L0_ffn_w_fc", [4096, D]), ("L0_ffn_w_proj", [D, 4096]),
                       ("L0_ffn_u", [4096, 16]), ("L0_ffn_v", [16, D]),
                       ("L0_ffn_fast_b", [16, 16]), ("L0_ffn_gate", [1, 1])):
            din(nm, sh)
        scr2fc = scrt("s2fcT", [D, 4096])
        scr2pr = scrt("s2prT", [4096, D])
    if "sl3" in names:
        for nm, sh in (("L0_attn_b_W_Q_T", [D, D]), ("L0_attn_b_W_K_T", [D, D]),
                       ("L0_attn_b_W_V_T", [D, D]), ("L0_attn_b_W_O_T", [D, D])):
            din(nm, sh)
        scr3 = [scrt(f"s3{i}", [D, D]) for i in range(4)]
    if "sl5" in names:
        for nm, sh in (("L1_attn_a_Wqkv", [D, 3 * D]), ("L1_attn_a_Wout", [D, D]),
                       ("L1_attn_a_gamma", [D]), ("L1_attn_a_beta", [D])):
            din(nm, sh)
        scr5q = scrt("s5q", [D, 3 * D])
        scr5o = scrt("s5o", [D, D])
    if "sl6" in names or "sl6b" in names:
        for nm, sh in (("L1_ffn_sim_matrix", [D, 16]), ("L1_ffn_gates", [16]),
                       ("L1_ffn_q_proj", [16 * D, 128]),
                       ("L1_ffn_k_proj", [16 * D, 128]),
                       ("L1_ffn_v_proj", [16 * D, 128]),
                       ("L1_ffn_o_proj", [16 * 128, D])):
            din(nm, sh)
        scr6 = {nm: scrt(f"s6{nm}", [16 * D, 128]) for nm in ("q", "k", "v")}
        scr6["o"] = scrt("s6o", [16 * 128, D])
    if "sl7" in names:
        for nm in ("L1_attn_b_Wq_T", "L1_attn_b_Wk_T", "L1_attn_b_Wv_T"):
            din(nm, [D, D])
        scr7 = [scrt(f"s7{i}", [D, D]) for i in range(3)]

    cmap = {}
    for name, (shape, dt) in CONST_SPECS.items():
        cmap[name] = nc.dram_tensor("c_" + name, shape, dt,
                                    kind="ExternalInput").ap()
    b = B(nc, es)
    b.load_consts(cmap)
    b.load_x(xin)
    tiles2 = None
    prep6 = None
    first = True
    for i in ids:
        sl = SEQ[i]
        if sl == "sl1":
            b.sl1(dr["L0_attn_a_Wq_T"], dr["L0_attn_a_bq"], dr["L0_attn_a_Wk_T"],
                  dr["L0_attn_a_bk"], dr["L0_attn_a_Wv_T"], dr["L0_attn_a_bv"],
                  dr["L0_attn_a_Wo_T"], dr["L0_attn_a_bo"], dr["L0_attn_a_sinks"],
                  scr1, first=first)
        elif sl in ("sl2", "sl2b"):
            if tiles2 is None:
                b.sl2_prep(dr["L0_ffn_w_fc"], dr["L0_ffn_w_proj"],
                           dr["L0_ffn_u"], dr["L0_ffn_v"],
                           dr["L0_ffn_fast_b"], dr["L0_ffn_gate"],
                           scr2fc, scr2pr)
                tiles2 = True
            b.sl2(scr2fc, scr2pr, first=first)
        elif sl == "sl3":
            b.sl3(dr["L0_attn_b_W_Q_T"], dr["L0_attn_b_W_K_T"],
                  dr["L0_attn_b_W_V_T"], dr["L0_attn_b_W_O_T"], scr3,
                  first=first)
        elif sl == "sl5":
            b.sl5(dr["L1_attn_a_Wqkv"], dr["L1_attn_a_Wout"],
                  dr["L1_attn_a_gamma"], dr["L1_attn_a_beta"],
                  scr5q, scr5o, first=first)
        elif sl in ("sl6", "sl6b"):
            if prep6 is None:
                prep6 = b.sl6_prep(dr["L1_ffn_sim_matrix"], dr["L1_ffn_gates"],
                                   dr["L1_ffn_q_proj"], dr["L1_ffn_k_proj"],
                                   dr["L1_ffn_v_proj"], dr["L1_ffn_o_proj"],
                                   scr6)
            b.sl6(prep6[0], prep6[1], scr6, first=first)
        elif sl == "sl7":
            b.sl7(dr["L1_attn_b_Wq_T"], dr["L1_attn_b_Wk_T"],
                  dr["L1_attn_b_Wv_T"], scr7)
        first = False
    b.store_x(xout)
    b.P.replay()
    es.close()
    return nc, sorted(dr.keys())


LAUNCH_GROUPS = [[0, 1, 2, 3, 4, 5, 6, 7]]

_BUILD_CACHE = {}


def kernel(**inputs):
    consts = host_consts()
    x = np.ascontiguousarray(inputs["L0_attn_a_x"], dtype=np.float32)
    nco = 8
    reshape = {
        "L0_attn_b_W_Q": (D, D), "L0_attn_b_W_K": (D, D),
        "L0_attn_b_W_V": (D, D),
        "L1_ffn_q_proj": (16 * D, 128), "L1_ffn_k_proj": (16 * D, 128),
        "L1_ffn_v_proj": (16 * D, 128), "L1_ffn_o_proj": (16 * 128, D),
        "L0_ffn_gate": (1, 1),
    }
    cur = [x[c] for c in range(nco)]
    for gi, ids in enumerate(LAUNCH_GROUPS):
        key = tuple(ids)
        if key not in _BUILD_CACHE:
            _BUILD_CACHE[key] = build_launch(ids)
        nc, wnames = _BUILD_CACHE[key]
        tbase = {
            "L0_attn_a_Wq_T": "L0_attn_a_Wq", "L0_attn_a_Wk_T": "L0_attn_a_Wk",
            "L0_attn_a_Wv_T": "L0_attn_a_Wv", "L0_attn_a_Wo_T": "L0_attn_a_Wo",
            "L0_attn_b_W_Q_T": "L0_attn_b_W_Q",
            "L0_attn_b_W_K_T": "L0_attn_b_W_K",
            "L0_attn_b_W_V_T": "L0_attn_b_W_V",
            "L0_attn_b_W_O_T": "L0_attn_b_W_O",
            "L1_attn_b_Wq_T": "L1_attn_b_Wq", "L1_attn_b_Wk_T": "L1_attn_b_Wk",
            "L1_attn_b_Wv_T": "L1_attn_b_Wv",
        }
        shared = {}
        for nm in wnames:
            base = tbase[nm] if nm.endswith("_T") else nm
            a = np.asarray(inputs[base], dtype=np.float32)
            if base in reshape:
                a = a.reshape(reshape[base])
            if nm.endswith("_T"):
                a = a.T
            shared[nm] = np.ascontiguousarray(a)
        in_maps = []
        for c in range(nco):
            m = {"xin": np.ascontiguousarray(cur[c])}
            m.update(shared)
            for kk, vv in consts.items():
                m["c_" + kk] = vv
            in_maps.append(m)
        res = run_bass_kernel_spmd(nc, in_maps, core_ids=list(range(nco)))
        cur = [r["xout"] for r in res.results]
    return np.stack(cur, 0).astype(np.float32)


def build_sl1_launch():
    nc = bass.Bass("TRN2", target_bir_lowering=False)
    es = ExitStack()
    xin = nc.dram_tensor("xin", [T, D], F32, kind="ExternalInput").ap()
    xout = nc.dram_tensor("xout", [T, D], F32, kind="ExternalOutput").ap()
    wq = nc.dram_tensor("wq", [4096, D], F32, kind="ExternalInput").ap()
    bq = nc.dram_tensor("bq", [4096], F32, kind="ExternalInput").ap()
    wk = nc.dram_tensor("wk", [512, D], F32, kind="ExternalInput").ap()
    bk = nc.dram_tensor("bk", [512], F32, kind="ExternalInput").ap()
    wv = nc.dram_tensor("wv", [512, D], F32, kind="ExternalInput").ap()
    bv = nc.dram_tensor("bv", [512], F32, kind="ExternalInput").ap()
    wo = nc.dram_tensor("wo", [D, 4096], F32, kind="ExternalInput").ap()
    bo = nc.dram_tensor("bo", [D], F32, kind="ExternalInput").ap()
    sinks = nc.dram_tensor("sinks", [64], F32, kind="ExternalInput").ap()
    scr = {"q": nc.dram_tensor("scr_q", [4096, D], BF16, kind="Internal").ap(),
           "k": nc.dram_tensor("scr_k", [512, D], BF16, kind="Internal").ap(),
           "v": nc.dram_tensor("scr_v", [512, D], BF16, kind="Internal").ap(),
           "o": nc.dram_tensor("scr_o", [D, 4096], BF16, kind="Internal").ap()}
    cmap = {}
    for name, (shape, dt) in CONST_SPECS.items():
        cmap[name] = nc.dram_tensor("c_" + name, shape, dt, kind="ExternalInput").ap()
    b = B(nc, es)
    b.load_consts(cmap)
    b.load_x(xin)
    b.sl1(wq, bq, wk, bk, wv, bv, wo, bo, sinks, scr, first=True)
    b.store_x(xout)
    b.P.replay()
    es.close()
    return nc


def build_sl6_launch():
    nc = bass.Bass("TRN2", target_bir_lowering=False)
    es = ExitStack()
    xin = nc.dram_tensor("xin", [T, D], F32, kind="ExternalInput").ap()
    xout = nc.dram_tensor("xout", [T, D], F32, kind="ExternalOutput").ap()
    sim_m = nc.dram_tensor("sim_m", [D, 16], F32, kind="ExternalInput").ap()
    gates = nc.dram_tensor("gates", [16], F32, kind="ExternalInput").ap()
    wq = nc.dram_tensor("wq", [16 * D, 128], F32, kind="ExternalInput").ap()
    wk = nc.dram_tensor("wk", [16 * D, 128], F32, kind="ExternalInput").ap()
    wv = nc.dram_tensor("wv", [16 * D, 128], F32, kind="ExternalInput").ap()
    wo = nc.dram_tensor("wo", [16 * 128, D], F32, kind="ExternalInput").ap()
    scr = {nm: nc.dram_tensor(f"scr_{nm}", [16 * D, 128], BF16, kind="Internal").ap()
           for nm in ("q", "k", "v")}
    scr["o"] = nc.dram_tensor("scr_o", [16 * 128, D], BF16, kind="Internal").ap()
    cmap = {}
    for name, (shape, dt) in CONST_SPECS.items():
        cmap[name] = nc.dram_tensor("c_" + name, shape, dt, kind="ExternalInput").ap()
    b = B(nc, es)
    b.load_consts(cmap)
    b.load_x(xin)
    simn, sgbc = b.sl6_prep(sim_m, gates, wq, wk, wv, wo, scr)
    b.sl6(simn, sgbc, scr, first=True)
    b.store_x(xout)
    b.P.replay()
    es.close()
    return nc


def build_sl5_launch():
    nc = bass.Bass("TRN2", target_bir_lowering=False)
    es = ExitStack()
    xin = nc.dram_tensor("xin", [T, D], F32, kind="ExternalInput").ap()
    xout = nc.dram_tensor("xout", [T, D], F32, kind="ExternalOutput").ap()
    wqkv = nc.dram_tensor("wqkv", [D, 3 * D], F32, kind="ExternalInput").ap()
    wout = nc.dram_tensor("wout", [D, D], F32, kind="ExternalInput").ap()
    gamma = nc.dram_tensor("gamma", [D], F32, kind="ExternalInput").ap()
    beta = nc.dram_tensor("beta", [D], F32, kind="ExternalInput").ap()
    scr_qkv = nc.dram_tensor("scr_qkv", [D, 3 * D], BF16, kind="Internal").ap()
    scr_out = nc.dram_tensor("scr_out", [D, D], BF16, kind="Internal").ap()
    cmap = {}
    for name, (shape, dt) in CONST_SPECS.items():
        cmap[name] = nc.dram_tensor("c_" + name, shape, dt, kind="ExternalInput").ap()
    b = B(nc, es)
    b.load_consts(cmap)
    b.load_x(xin)
    b.sl5(wqkv, wout, gamma, beta, scr_qkv, scr_out, first=True,
          upto=getattr(B, "DBG_SL5_UPTO", "full") or "full")
    b.store_x(xout)
    b.P.replay()
    es.close()
    return nc


def build_sl3_launch():
    nc = bass.Bass("TRN2", target_bir_lowering=False)
    es = ExitStack()
    xin = nc.dram_tensor("xin", [T, D], F32, kind="ExternalInput").ap()
    xout = nc.dram_tensor("xout", [T, D], F32, kind="ExternalOutput").ap()
    wq = nc.dram_tensor("wq", [D, D], F32, kind="ExternalInput").ap()
    wk = nc.dram_tensor("wk", [D, D], F32, kind="ExternalInput").ap()
    wv = nc.dram_tensor("wv", [D, D], F32, kind="ExternalInput").ap()
    wo = nc.dram_tensor("wo", [D, D], F32, kind="ExternalInput").ap()
    scr = [nc.dram_tensor(f"scr{i}", [D, D], BF16, kind="Internal").ap()
           for i in range(4)]
    cmap = {}
    for name, (shape, dt) in CONST_SPECS.items():
        cmap[name] = nc.dram_tensor("c_" + name, shape, dt, kind="ExternalInput").ap()
    b = B(nc, es)
    b.load_consts(cmap)
    b.load_x(xin)
    b.sl3(wq, wk, wv, wo, scr, first=True)
    b.store_x(xout)
    b.P.replay()
    es.close()
    return nc


def build_sl7_launch(upto="full"):
    nc = bass.Bass("TRN2", target_bir_lowering=False)
    es = ExitStack()
    xin = nc.dram_tensor("xin", [T, D], F32, kind="ExternalInput").ap()
    xout = nc.dram_tensor("xout", [T, D], F32, kind="ExternalOutput").ap()
    wq = nc.dram_tensor("wq", [D, D], F32, kind="ExternalInput").ap()
    wk = nc.dram_tensor("wk", [D, D], F32, kind="ExternalInput").ap()
    wv = nc.dram_tensor("wv", [D, D], F32, kind="ExternalInput").ap()
    scr = [nc.dram_tensor(f"scr{i}", [D, D], BF16, kind="Internal").ap()
           for i in range(3)]
    cmap = {}
    for name, (shape, dt) in CONST_SPECS.items():
        cmap[name] = nc.dram_tensor("c_" + name, shape, dt, kind="ExternalInput").ap()
    b = B(nc, es)
    b.load_consts(cmap)
    b.load_x(xin)
    b.sl7(wq, wk, wv, scr, upto=upto)
    b.store_x(xout)
    b.P.replay()
    es.close()
    return nc




# version 15
# speedup vs baseline: 1.0218x; 1.0218x over previous
import numpy as np
import ml_dtypes
from contextlib import ExitStack
import concourse.bass as bass
import concourse.mybir as mybir
from concourse.bass_utils import run_bass_kernel_spmd

F32 = mybir.dt.float32
BF16 = mybir.dt.bfloat16
AF = mybir.ActivationFunctionType
ALU = mybir.AluOpType
AX = mybir.AxisListType
bfdt = ml_dtypes.bfloat16

T = 2048
D = 1024
TT = T // 128
FC = D // 128
NEG = -1.0e30
LN_EPS = 1e-5

ENGS = ("tensor", "vector", "scalar", "gpsimd", "sync")


class Phaser:
    def __init__(self, nc, es):
        self.nc = nc
        self.SP = es.enter_context(nc.semaphore("SP"))
        self._dma_sems = {}
        self._dma_counts = {}
        self._es = es
        self.phases = []
        self.cur = {e: [] for e in ENGS}
        self.cur_needs = []

    def dma_count(self, name):
        return self._dma_counts.get(name, 0)

    def _sem(self, name):
        if name not in self._dma_sems:
            self._dma_sems[name] = self._es.enter_context(self.nc.semaphore(name))
            self._dma_counts[name] = 0
        return self._dma_sems[name]

    def op(self, eng, fn):
        self.cur[eng].append(("op", fn))

    def dma(self, eng, sem_name, fn):
        sem = self._sem(sem_name)
        self._dma_counts[sem_name] += 16
        self.cur[eng].append(("dma", fn, sem))
        return self._dma_counts[sem_name]

    def need(self, sem_name, target):
        if target > 0:
            self._sem(sem_name)
            self.cur_needs.append((sem_name, target))

    def phase(self):
        if any(self.cur[e] for e in ENGS):
            self.phases.append((self.cur, self.cur_needs))
            self.cur = {e: [] for e in ENGS}
            self.cur_needs = []
        else:
            assert not self.cur_needs, "needs declared in empty phase"

    def replay(self):
        self.phase()
        barrier_before = []
        tot = 0
        for ops, _ in self.phases:
            barrier_before.append(tot)
            tot += sum(1 for e in ENGS if ops[e])
        nc = self.nc
        with nc.Block() as blk:
            for eng_name in ENGS:
                deco = getattr(blk, eng_name)

                def make_body(ename):
                    def body(eng):
                        for pi, (ops, needs) in enumerate(self.phases):
                            items = ops[ename]
                            if not items:
                                continue
                            if barrier_before[pi] > 0:
                                eng.wait_ge(self.SP, barrier_before[pi])
                            for sem_name, tgt in needs:
                                eng.wait_ge(self._dma_sems[sem_name], tgt)
                            inst = None
                            last_was_dma = True
                            for item in items:
                                if item[0] == "op":
                                    inst = item[1](eng)
                                    last_was_dma = False
                                else:
                                    di = item[1](eng)
                                    di.then_inc(item[2], 16)
                                    last_was_dma = True
                            if last_was_dma or inst is None:
                                eng.sem_inc(self.SP, 1)
                            else:
                                inst.then_inc(self.SP, 1)
                    return body

                deco(make_body(eng_name))


class Arena:
    def __init__(self, ap, name):
        self.ap = ap
        self.n = ap.shape[1]
        self.off = 0
        self.name = name
        self.marks = []

    def take(self, cols):
        assert self.off + cols <= self.n, (
            f"arena {self.name} overflow: {self.off}+{cols} > {self.n}")
        sl = self.ap[:, self.off:self.off + cols]
        self.off += cols
        return sl

    def take3(self, mid, inner):
        return self.take(mid * inner).rearrange("p (a b) -> p a b", a=mid)

    def push(self):
        self.marks.append(self.off)

    def pop(self):
        self.off = self.marks.pop()


class B:
    DBG_NO_EVAC = False
    DBG_NO_TRANS = False
    DBG_NQT = 4
    DBG_NO_XADD = False
    DBG_QT1_UPTO = None
    DBG_NO_CAST = False
    DBG_NO_GP = False
    def __init__(self, nc, es, n_bf=50688, n_f32=768):
        self.nc = nc
        self.P = Phaser(nc, es)
        self.X = nc.alloc_sbuf_tensor("X", [128, TT * D], F32).ap()
        self.xn_fm = nc.alloc_sbuf_tensor("xn_fm", [128, FC * T], BF16).ap()
        self.abf = Arena(nc.alloc_sbuf_tensor("abf", [128, n_bf], BF16).ap(), "bf")
        self.af32 = Arena(nc.alloc_sbuf_tensor("af32", [128, n_f32], F32).ap(), "f32")
        self.ident_bf = nc.alloc_sbuf_tensor("ident_bf", [128, 128], BF16).ap()
        self.ident_f32 = nc.alloc_sbuf_tensor("ident_f32", [128, 128], F32).ap()
        self.mask_w = nc.alloc_sbuf_tensor("mask_w", [128, 256], BF16).ap()
        self.mask256 = nc.alloc_sbuf_tensor("mask256", [128, 256], BF16).ap()
        self.maskpad = nc.alloc_sbuf_tensor("maskpad", [128, 1024], BF16).ap()
        self.ones_bf = nc.alloc_sbuf_tensor("ones_bf", [128, 128], BF16).ap()
        self.ones_f32 = nc.alloc_sbuf_tensor("ones_f32", [1, 128], F32).ap()
        self.onec_f32 = nc.alloc_sbuf_tensor("onec_f32", [128, 1], F32).ap()
        self.pbig = nc.alloc_psum_tensor("pbig", [128, 2048], F32).ap()
        self.pb = ([self.pbig[:, i * 512:(i + 1) * 512] for i in range(4)]
                   + [nc.alloc_psum_tensor(f"pb{i}", [128, 512], F32).ap()
                      for i in (4, 5)])
        self.pt = [nc.alloc_psum_tensor(f"pt{i}", [128, 1024], BF16).ap()
                   for i in range(2)]
        self.ln_rstd = nc.alloc_sbuf_tensor("ln_rstd", [128, TT], F32).ap()
        self.ln_b = nc.alloc_sbuf_tensor("ln_b", [128, TT], F32).ap()
        self.stat = nc.alloc_sbuf_tensor("stat", [128, 64], F32).ap()
        self.xtbuf = nc.alloc_sbuf_tensor("xtbuf", [128, 2, D], BF16).ap()
        self._evac_flip = 0

    @property
    def mask_c(self):
        return self.mask_w[:, 128:256]

    def ptv(self, i):
        return self.pt[i]

    def evac_eng(self):
        self._evac_flip ^= 1
        return "vector" if self._evac_flip else "scalar"

    def _evac(self, e, dst, src, scale=1.0):
        if hasattr(e, "activation"):
            return e.activation(dst, src, AF.Copy, scale=scale)
        if scale == 1.0:
            return e.tensor_copy(dst, src)
        return e.tensor_scalar(dst, src, scale, None, ALU.mult)


    def load_consts(self, cmap):
        P = self.P
        for attr, ap in cmap.items():
            dst = getattr(self, attr)
            P.dma("sync", "dC", lambda e, d=dst, s=ap: e.dma_start(out=d, in_=s))
        self.const_tgt = P.dma_count("dC")

    def load_x(self, x_dram):
        P = self.P
        xr = x_dram.rearrange("(t p) d -> p t d", p=128)
        self.x_tgt = P.dma("sync", "dX", lambda e, s=xr: e.dma_start(
            out=self.X.rearrange("p (t d) -> p t d", t=TT), in_=s))

    def store_x(self, out_dram):
        P = self.P
        P.phase()
        xr = out_dram.rearrange("(t p) d -> p t d", p=128)
        tgt = P.dma("sync", "dOut", lambda e, d=xr: e.dma_start(
            out=d, in_=self.X.rearrange("p (t d) -> p t d", t=TT)))
        P.phase()
        P.need("dOut", tgt)
        P.op("sync", lambda e: e.nop())
        P.phase()

    def transpose_waves(self, items, f32=False):
        P = self.P
        ident = self.ident_f32 if f32 else self.ident_bf
        wave = 4 if f32 else 8
        pend = None
        for c0 in range(0, len(items), wave):
            chunk = items[c0:c0 + wave]
            w = (c0 // wave) % 2
            for i, (src, dst) in enumerate(chunk):
                if f32:
                    slot = self.pb[4 + w][:, i * 128:(i + 1) * 128]
                else:
                    slot = self.ptv(w)[:, i * 128:(i + 1) * 128]
                if not self.DBG_NO_TRANS:
                    P.op("tensor", lambda e, s=src, o=slot, idn=ident:
                         e.transpose(o, s, idn))
            if pend is not None and not self.DBG_NO_EVAC:
                eng = "vector" if pend[1] == 0 else "scalar"
                for i, (src, dst) in enumerate(pend[0]):
                    if f32:
                        slot = self.pb[4 + pend[1]][:, i * 128:(i + 1) * 128]
                    else:
                        slot = self.ptv(pend[1])[:, i * 128:(i + 1) * 128]
                    P.op(eng, lambda e, o=dst, s=slot: e.tensor_copy(o, s)
                         if hasattr(e, "tensor_copy") else e.copy(o, s))
            pend = (chunk, w)
            P.phase()
        if pend is not None:
            eng = "vector" if pend[1] == 0 else "scalar"
            for i, (src, dst) in enumerate(pend[0]):
                if f32:
                    slot = self.pb[4 + pend[1]][:, i * 128:(i + 1) * 128]
                else:
                    slot = self.ptv(pend[1])[:, i * 128:(i + 1) * 128]
                P.op(eng, lambda e, o=dst, s=slot: e.tensor_copy(o, s)
                     if hasattr(e, "tensor_copy") else e.copy(o, s))
            P.phase()

    def emit_ln(self, needs=(), upto="full"):
        P = self.P
        st = self.stat
        ssum = st[:, 0:16]
        ssq = st[:, 16:32]
        for g in range(TT // 2):
            if g == 0:
                for nm, tgt in needs:
                    P.need(nm, tgt)
            for i in range(2):
                t = 2 * g + i
                xt = self.X[:, t * D:(t + 1) * D]
                P.op("vector", lambda e, x=xt, c=ssum[:, t:t + 1]:
                     e.reduce_sum(c, x, axis=AX.X))
                P.op("scalar", lambda e, x=xt, c=ssq[:, t:t + 1], i=i:
                     e.activation(self.xtbuf[:, i, :], x, AF.Square,
                                  accum_out=c))
            P.phase()
        mu = st[:, 32:48]
        var = st[:, 48:64]
        q2 = st[:, 0:16]
        P.op("vector", lambda e: e.tensor_scalar_mul(mu, ssum, 1.0 / D))
        P.op("vector", lambda e: e.tensor_scalar_mul(self.ln_b, ssq, 1.0 / D))
        P.phase()
        P.op("vector", lambda e: e.scalar_tensor_tensor(
            var, mu, -1.0, mu, ALU.mult, ALU.mult))
        P.phase()
        P.op("vector", lambda e: e.scalar_tensor_tensor(
            var, var, LN_EPS, self.ln_b, ALU.add, ALU.add))
        P.phase()
        if upto == "lnstat":
            return
        P.op("scalar", lambda e: e.activation(self.ln_rstd, var, AF.Sqrt))
        P.phase()
        P.op("vector", lambda e: e.reciprocal(self.ln_rstd, self.ln_rstd))
        P.op("vector", lambda e: e.tensor_scalar_mul(self.ln_b, mu, -1.0))
        P.phase()
        if upto == "lncol":
            return
        for g in range(TT // 2):
            for i in range(2):
                t = 2 * g + i
                eng = "vector"
                P.op(eng, lambda e, t=t, i=i: e.tensor_scalar(
                    self.xtbuf[:, i, :], self.X[:, t * D:(t + 1) * D],
                    self.ln_b[:, t:t + 1], self.ln_rstd[:, t:t + 1],
                    ALU.add, ALU.mult))
            P.phase()
            if upto == "lnapply" and g == 0:
                return
            items = []
            for i in range(2):
                t = 2 * g + i
                for fc in range(FC):
                    items.append((self.xtbuf[:, i, fc * 128:(fc + 1) * 128],
                                  self.xn_fm[:, fc * T + t * 128: fc * T + (t + 1) * 128]))
            self.transpose_waves(items)
            if upto == f"lng{g}":
                return

    def xn_chunk(self, ic):
        return self.xn_fm[:, ic * T:(ic + 1) * T]

    def mm_waves(self, groups, evac_engines=None):
        P = self.P
        pend = None
        for c0 in range(0, len(groups), 3):
            chunk = groups[c0:c0 + 3]
            w = (c0 // 3) % 2
            base = 0 if w == 0 else 3
            for i, (mm, _) in enumerate(chunk):
                P.op("tensor", lambda e, f=mm, p=self.pb[base + i]: f(e, p))
            if pend is not None:
                pb0 = 0 if pend[1] == 0 else 3
                for i, (_, ev) in enumerate(pend[0]):
                    eng = (evac_engines[i % len(evac_engines)]
                           if evac_engines else self.evac_eng())
                    P.op(eng, lambda e, f=ev, p=self.pb[pb0 + i]: f(e, p))
            pend = (chunk, w)
            P.phase()
        if pend is not None:
            pb0 = 0 if pend[1] == 0 else 3
            for i, (_, ev) in enumerate(pend[0]):
                eng = (evac_engines[i % len(evac_engines)]
                       if evac_engines else self.evac_eng())
                P.op(eng, lambda e, f=ev, p=self.pb[pb0 + i]: f(e, p))
            P.phase()

    def prep_wt(self, w_dram, scratch, n_in, n_out, tag):
        return self.P.dma("gpsimd", "dW",
                          lambda e, s=w_dram, d=scratch: e.dma_start(out=d, in_=s))

    def read_wt_stripes(self, scratch, wt_sb, n_in, n_out):
        return self.P.dma("sync", "dT", lambda e, s=scratch, d=wt_sb:
                          e.dma_start(out=d,
                                      in_=s.rearrange("(c p) n -> p c n", p=128)))


    def sl7(self, wq, wk, wv, scr, upto="full"):
        nc, P = self.nc, self.P
        wts = [wq, wk, wv]
        if not self.DBG_NO_CAST:
            cast_tgts = [self.prep_wt(wts[i], scr[i], D, D, f"w{i}")
                         for i in range(3)]
        P.phase()
        if upto == "none":
            return
        self.emit_ln(needs=[("dX", self.x_tgt), ("dC", self.const_tgt)],
                     upto=(upto if upto.startswith("ln") else "full"))
        if upto.startswith("ln"):
            return
        self.abf.push()
        wtbuf = self.abf.take3(8, D)
        q_half = self.abf.take3(8, T // 4)
        k_fm = self.abf.take3(8, T)
        v_tok = self.abf.take3(TT, D)
        P_row = self.abf.take(T)
        P_T = self.abf.take3(TT, 128)

        P.need("dW", cast_tgts[2])
        tgt_k = self.read_wt_stripes(scr[1], wtbuf, D, D)
        P.phase()
        self._proj_fm(k_fm, wtbuf, tgt_k, scale=1.0)
        if upto == "kproj":
            return
        tgt_v = self.read_wt_stripes(scr[2], wtbuf, D, D)
        P.phase()
        self._proj_tok(v_tok, wtbuf, tgt_v)
        if upto == "vproj":
            return
        for quar in range(4):
            tgt_q = self.read_wt_stripes(scr[0], wtbuf, D, D)
            P.phase()
            self._proj_fm(q_half, wtbuf, tgt_q, scale=1.0 / 32.0,
                          tqs=[quar], toff=quar * 512)
            if upto == "qproj":
                return
            self._attn_sl7(q_half, k_fm, v_tok, P_row, P_T,
                           range(quar * 4, quar * 4 + self.DBG_NQT), quar * 512)
            if upto == "attn1":
                return
        self.abf.pop()

    def _attn_sl7(self, q_half, k_fm, v_tok, P_row, P_T, qts, toff):
        P = self.P
        st = self.stat
        for qt in qts:
            stop_at = self.DBG_QT1_UPTO if qt > min(qts) else None
            ksp = (qt + 1) * 128
            nb = (ksp + 511) // 512
            dkb = qt // 4
            doff = (qt % 4) * 128
            q0 = qt * 128 - toff
            for kb in range(nb):
                N = min(512, ksp - kb * 512)
                def mk(e, kb=kb, N=N, q0=q0, doff=doff, diag=(kb == dkb)):
                    if diag:
                        e.matmul(self.pb[kb][:, 0:N], self.ident_bf,
                                 self.maskpad[:, 384 - doff:384 - doff + N],
                                 start=True, stop=False)
                    inst = None
                    for ic in range(FC):
                        inst = e.matmul(
                            self.pb[kb][:, 0:N],
                            q_half[:, ic, q0:q0 + 128],
                            k_fm[:, ic, kb * 512:kb * 512 + N],
                            start=(ic == 0 and not diag),
                            stop=(ic == FC - 1))
                    return inst
                P.op("tensor", mk)
            P.phase()
            if stop_at == "s":
                return
            P.op("vector", lambda e, ksp=ksp: e.reduce_max(
                st[:, 9:10], self.pbig[:, 0:ksp], axis=AX.X, negate=True))
            P.phase()
            if stop_at == "stats":
                return
            P.op("scalar", lambda e, ksp=ksp: e.activation(
                P_row[:, 0:ksp], self.pbig[:, 0:ksp], AF.Exp,
                bias=st[:, 9:10], scale=1.0, accum_out=st[:, 24:25]))
            P.phase()
            P.op("vector", lambda e: e.reciprocal(st[:, 25:26], st[:, 24:25]))
            P.phase()
            P.op("vector", lambda e, ksp=ksp: e.tensor_scalar(
                P_row[:, 0:ksp], P_row[:, 0:ksp], st[:, 25:26], None, ALU.mult))
            P.phase()
            if stop_at == "norm":
                return
            items = [(P_row[:, kt * 128:(kt + 1) * 128], P_T[:, kt, :])
                     for kt in range(qt + 1)]
            self.transpose_waves(items)
            if stop_at == "trans":
                return
            for dh in range(2):
                def mkpv(e, dh=dh, qt=qt):
                    inst = None
                    for kt in range(qt + 1):
                        inst = e.matmul(self.pb[dh], P_T[:, kt, :],
                                        v_tok[:, kt, dh * 512:(dh + 1) * 512],
                                        start=(kt == 0), stop=(kt == qt))
                    return inst
                P.op("tensor", mkpv)
            P.phase()
            if not (self.DBG_NO_XADD and qt > 0):
                for dh in range(2):
                    xs = self.X[:, qt * D + dh * 512: qt * D + (dh + 1) * 512]
                    P.op("vector", lambda e, xs=xs, dh=dh: e.tensor_tensor(
                        xs, xs, self.pb[dh], ALU.add))
                P.phase()


    def sl2_prep(self, w_fc, w_proj, u, v, fb, gate, scr_fc, scr_pr):
        P = self.P
        self._sl2_small_args = (u, v, fb, gate)
        self.abf.push()
        if True:
            pass
        self.abf.pop()
        self.abf.push()
        self.abf.push()
        W32 = self.abf.take(8192).bitcast(F32)
        T32 = self.abf.take(2048).bitcast(F32)
        SG32 = self.abf.take(2048).bitcast(F32)
        OBF = self.abf.take(1024)
        TBF = self.abf.take3(8, 128)
        st = self.stat
        scol = st[:, 40:41]
        thcol = st[:, 41:42]
        for (wsrc, scrdst, nrows, ncols) in [(w_fc, scr_fc, 4096, 1024),
                                             (w_proj, scr_pr, 1024, 4096)]:
            nch = nrows // 128
            nsub = ncols // 1024
            for ch in range(nch):
                tload = P.dma("sync", "dWL", lambda e, ch=ch, wsrc=wsrc,
                              ncols=ncols: e.dma_start(
                    out=W32[:, 0:ncols],
                    in_=wsrc[ch * 128:(ch + 1) * 128, :]))
                P.phase()
                P.need("dWL", tload)
                P.op("vector", lambda e, ncols=ncols: e.reduce_sum(
                    scol, W32[:, 0:ncols], axis=AX.X, apply_absolute_value=True))
                P.phase()
                P.op("vector", lambda e, n=ncols: e.tensor_scalar(
                    st[:, 42:43], scol, 1.0 / n, 1e-5, ALU.mult, ALU.max))
                P.op("vector", lambda e, n=ncols: e.tensor_scalar(
                    thcol, scol, 0.5 / n, 0.5e-5, ALU.mult, ALU.max))
                P.phase()
                for sub in range(nsub):
                    wsl = W32[:, sub * 1024:(sub + 1) * 1024]
                    P.op("scalar", lambda e, wsl=wsl: e.sign(SG32, wsl))
                    P.op("scalar", lambda e, wsl=wsl: e.activation(
                        T32, wsl, AF.Abs))
                    P.phase()
                    P.op("vector", lambda e: e.scalar_tensor_tensor(
                        T32, T32, thcol, SG32, ALU.is_gt, ALU.mult))
                    P.phase()
                    P.op("vector", lambda e: e.tensor_scalar(
                        OBF, T32, st[:, 42:43], None, ALU.mult))
                    P.phase()
                    self.transpose_waves(
                        [(OBF[:, fc * 128:(fc + 1) * 128], TBF[:, fc, :])
                         for fc in range(FC)])
                    tst = P.dma("sync", "dWS", lambda e, ch=ch, sub=sub,
                                scrdst=scrdst: e.dma_start(
                        out=scrdst.rearrange("(f p) m -> p f m", p=128)[
                            :, sub * 8:sub * 8 + 8, ch * 128:(ch + 1) * 128],
                        in_=TBF))
                    P.phase()
                    P.need("dWS", tst)
                    P.op("sync", lambda e: e.nop())
                    P.phase()
        self.abf.pop()

    def _sl2_small(self):
        P = self.P
        u, v, fb, gate = self._sl2_small_args
        vT = self.abf.take3(8, 16)
        uT = self.abf.take(4096)
        fbT = self.abf.take(16)
        gatecol = self.af32.take(1)
        self.abf.push()
        small = self.abf.take(1024 + 512 + 16 + 8)
        v_sb = small[:, 0:1024]
        u_sb = small[:, 0:512].rearrange("p (a b) -> p a b", a=32)
        fb_sb = small[:, 1024:1040]
        gt_sb = small[:, 1040:1041]
        t_small = None
        P.phase()
        t_small = P.dma("gpsimd", "dS2", lambda e: e.dma_start(
            out=v_sb[0:16, :], in_=v))
        t_small = P.dma("gpsimd", "dS2", lambda e: e.dma_start(
            out=fb_sb[0:16, :], in_=fb))
        t_small = P.dma("gpsimd", "dS2", lambda e: e.dma_start(
            out=gt_sb[0:1, :], in_=gate))
        P.phase()
        P.need("dS2", P.dma_count("dS2"))
        P.need("dC", self.const_tgt)
        for ic in range(FC):
            P.op("tensor", lambda e, ic=ic: e.transpose(
                self.ptv(0)[:, ic * 16:(ic + 1) * 16],
                v_sb[0:16, ic * 128:(ic + 1) * 128], self.ident_bf[0:16, 0:16]))
        P.op("tensor", lambda e: e.transpose(
            self.ptv(1)[0:16, 0:16], fb_sb[0:16, :], self.ident_bf[0:16, 0:16]))
        P.op("tensor", lambda e: e.matmul(
            self.pb[0][0:16, 0:1], self.ones_bf[0:1, 0:16], gt_sb[0:1, :],
            start=True, stop=True))
        P.phase()
        P.op("vector", lambda e: e.tensor_copy(
            vT.rearrange("p a b -> p (a b)"), self.ptv(0)[:, 0:128]))
        P.op("scalar", lambda e: e.copy(fbT[0:16, :], self.ptv(1)[0:16, 0:16]))
        P.op("vector", lambda e: e.tensor_copy(gatecol[0:16, :], self.pb[0][0:16, 0:1]))
        P.phase()
        t_u = P.dma("gpsimd", "dS2", lambda e: e.dma_start(
            out=u_sb, in_=u.rearrange("(c p) e -> p c e", p=128)))
        P.phase()
        P.need("dS2", P.dma_count("dS2"))
        for c in range(32):
            w = c % 2
            P.op("tensor", lambda e, c=c, w=w: e.transpose(
                self.ptv(w)[0:16, (c // 2 % 4) * 128:(c // 2 % 4 + 1) * 128],
                u_sb[:, c, :], self.ident_bf))
            if c % 8 == 7:
                P.phase()
                for cc in range(c - 7, c + 1):
                    ww = cc % 2
                    P.op("vector" if ww == 0 else "scalar",
                         lambda e, cc=cc, ww=ww: e.tensor_copy(
                             uT[0:16, cc * 128:(cc + 1) * 128],
                             self.ptv(ww)[0:16, (cc // 2 % 4) * 128:(cc // 2 % 4 + 1) * 128])
                         if hasattr(e, "tensor_copy") else e.copy(
                             uT[0:16, cc * 128:(cc + 1) * 128],
                             self.ptv(ww)[0:16, (cc // 2 % 4) * 128:(cc // 2 % 4 + 1) * 128]))
                P.phase()
        self.abf.pop()
        return vT, uT, fbT, gatecol

    def sl2(self, scr_fc, scr_pr, first=False):
        P = self.P
        needs = [("dX", self.x_tgt), ("dC", self.const_tgt)] if first else []
        self.emit_ln(needs=needs)
        self.abf.push()
        self.af32.push()
        vT, uT, fbT, gatecol = self._sl2_small()
        r2 = self.abf.take3(32, 512)
        w1buf = [self.abf.take3(8, 128) for _ in range(3)]
        w2buf = [self.abf.take(512) for _ in range(4)]
        t1 = self.abf.take(2048)
        t2g = self.abf.take(2048)
        rbuf = [self.abf.take(1024).bitcast(F32),
                self.abf.take(1024).bitcast(F32)]
        for tq in range(4):
            def mkt1(e, tq=tq):
                inst = None
                for ic in range(FC):
                    inst = e.matmul(self.pb[0][0:16, 0:512], vT[:, ic, :],
                                    self.xn_chunk(ic)[:, tq * 512:(tq + 1) * 512],
                                    start=(ic == 0), stop=(ic == FC - 1))
                return inst
            P.op("tensor", mkt1)
            P.phase()
            P.op("vector", lambda e, tq=tq: e.tensor_copy(
                t1[0:16, tq * 512:(tq + 1) * 512], self.pb[0][0:16, 0:512]))
            P.phase()
        for tq in range(4):
            P.op("tensor", lambda e, tq=tq: e.matmul(
                self.pb[0][0:16, 0:512], fbT[0:16, :],
                t1[0:16, tq * 512:(tq + 1) * 512], start=True, stop=True))
            P.phase()
            P.op("vector", lambda e, tq=tq: e.tensor_scalar(
                t2g[0:16, tq * 512:(tq + 1) * 512], self.pb[0][0:16, 0:512],
                gatecol[0:16, :], None, ALU.mult))
            P.phase()
        def issue_w1(oc):
            w = oc % 3
            return P.dma("sync", f"dW1{w}", lambda e, oc=oc, w=w: e.dma_start(
                out=w1buf[w],
                in_=scr_fc.rearrange("(c p) n -> p c n", p=128)[
                    :, :, oc * 128:(oc + 1) * 128]))

        def issue_w2(oc, dh):
            w = oc % 4
            return P.dma("sync", f"dW2{w}", lambda e, oc=oc, dh=dh, w=w:
                         e.dma_start(
                out=w2buf[w], in_=scr_pr[oc * 128:(oc + 1) * 128,
                                         dh * 512:(dh + 1) * 512]))

        for quar in range(4):
            w1tgt = {0: issue_w1(0), 1: issue_w1(1)}
            P.phase()
            for oc in range(34):
                if oc < 32:
                    pbk = 4 + (oc % 2)
                    P.need(f"dW1{oc % 3}", w1tgt[oc])
                    def mkh(e, oc=oc, quar=quar, pbk=pbk):
                        for ic in range(FC):
                            e.matmul(self.pb[pbk], w1buf[oc % 3][:, ic, :],
                                     self.xn_chunk(ic)[:, quar * 512:(quar + 1) * 512],
                                     start=(ic == 0), stop=False)
                        return e.matmul(
                            self.pb[pbk], uT[0:16, oc * 128:(oc + 1) * 128],
                            t2g[0:16, quar * 512:(quar + 1) * 512],
                            start=False, stop=True)
                    P.op("tensor", mkh)
                    if oc + 2 < 32:
                        w1tgt[oc + 2] = issue_w1(oc + 2)
                if 0 <= oc - 1 < 32:
                    o1 = oc - 1
                    P.op("vector", lambda e, o1=o1: e.tensor_scalar(
                        rbuf[o1 % 2], self.pb[4 + (o1 % 2)], 0.0, None, ALU.max))
                if 0 <= oc - 2 < 32:
                    o2 = oc - 2
                    P.op("scalar", lambda e, o2=o2: e.activation(
                        r2[:, o2, :], rbuf[o2 % 2], AF.Square))
                P.phase()
            for dh in range(2):
                w2tgt = {0: issue_w2(0, dh), 1: issue_w2(1, dh)}
                P.phase()
                for oc0 in range(0, 32, 2):
                    P.need(f"dW2{oc0 % 4}", w2tgt[oc0])
                    P.need(f"dW2{(oc0 + 1) % 4}", w2tgt[oc0 + 1])
                    def mko(e, oc0=oc0):
                        inst = None
                        for oc in (oc0, oc0 + 1):
                            for tl in range(4):
                                inst = e.matmul(
                                    self.pb[tl],
                                    r2[:, oc, tl * 128:(tl + 1) * 128],
                                    w2buf[oc % 4],
                                    start=(oc == 0), stop=(oc == 31))
                        return inst
                    P.op("tensor", mko)
                    for nx in (oc0 + 2, oc0 + 3):
                        if nx < 32:
                            w2tgt[nx] = issue_w2(nx, dh)
                    P.phase()
                for tl in range(4):
                    t = quar * 4 + tl
                    xs = self.X[:, t * D + dh * 512: t * D + (dh + 1) * 512]
                    P.op("vector", lambda e, xs=xs, tl=tl: e.tensor_tensor(
                        xs, xs, self.pb[tl], ALU.add))
                P.phase()
        self.abf.pop()


    def sl1(self, wq, bq, wk, bk, wv, bv, wo, bo, sinks, scr, first=False):
        P = self.P
        for nm, src in (("q", wq), ("k", wk), ("v", wv), ("o", wo)):
            P.dma("gpsimd", "dW1c", lambda e, s=src, d=scr[nm]: e.dma_start(
                out=d, in_=s))
        cast_tgt = P.dma_count("dW1c")
        P.phase()
        needs = [("dX", self.x_tgt), ("dC", self.const_tgt)] if first else []
        self.emit_ln(needs=needs)
        self.abf.push()
        self.af32.push()
        bq8 = self.af32.take(32)
        bkc = self.af32.take(4)
        sinkbc = self.af32.take(64)
        negsinkrep = self.af32.take(256).rearrange(
            "p (j u) -> p j u", j=32)
        bobc = self.abf.take(2048).bitcast(F32)
        bvrow = self.abf.take(512)
        self.abf.push()
        brow = self.abf.take(2688).bitcast(F32)
        tb = P.dma("sync", "dB1", lambda e: e.dma_start(
            out=brow[0:32, 0:128], in_=bq.rearrange("(c p) -> c p", p=128)))
        tb = P.dma("sync", "dB1", lambda e: e.dma_start(
            out=brow[0:4, 128:256], in_=bk.rearrange("(c p) -> c p", p=128)))
        tb = P.dma("sync", "dB1", lambda e: e.dma_start(
            out=brow[0:1, 256:320], in_=sinks.rearrange("(o s) -> o s", o=1)))
        tb = P.dma("sync", "dB1", lambda e: e.dma_start(
            out=brow[0:1, 320:1344], in_=bo.rearrange("(o s) -> o s", o=1)))
        tb = P.dma("gpsimd", "dB1g", lambda e: e.dma_start(
            out=bvrow[0:1, :], in_=bv.rearrange("(o s) -> o s", o=1)))
        P.phase()
        P.need("dB1", P.dma_count("dB1"))
        P.need("dB1g", P.dma_count("dB1g"))
        P.op("vector", lambda e: e.tensor_scalar_mul(
            brow[0:32, 0:128], brow[0:32, 0:128], 0.125))
        P.phase()
        P.op("tensor", lambda e: e.transpose(
            self.pb[0][:, 0:32], brow[0:32, 0:128], self.ident_f32[0:32, 0:32]))
        P.op("tensor", lambda e: e.transpose(
            self.pb[1][:, 0:4], brow[0:4, 128:256], self.ident_f32[0:4, 0:4]))
        P.op("tensor", lambda e: e.matmul(
            self.pb[2][:, 0:64], self.ones_f32, brow[0:1, 256:320],
            start=True, stop=True))
        for jj in range(2):
            P.op("tensor", lambda e, jj=jj: e.matmul(
                self.pb[3 + jj][:, 0:512], self.ones_f32,
                brow[0:1, 320 + jj * 512:320 + (jj + 1) * 512],
                start=True, stop=True))
        P.phase()
        P.op("vector", lambda e: e.tensor_copy(bq8, self.pb[0][:, 0:32]))
        P.op("vector", lambda e: e.tensor_copy(bkc, self.pb[1][:, 0:4]))
        P.op("scalar", lambda e: e.copy(sinkbc, self.pb[2][:, 0:64]))
        for jj in range(2):
            P.op("scalar", lambda e, jj=jj: e.copy(
                bobc[:, jj * 512:(jj + 1) * 512], self.pb[3 + jj][:, 0:512]))
        P.phase()
        for u in range(8):
            P.op("vector", lambda e, u=u: e.tensor_scalar_mul(
                negsinkrep[:, :, u],
                sinkbc.rearrange("p (j h) -> p h j", h=2)[:, u % 2, :], -1.0))
        P.phase()
        self.abf.pop()
        k_fm = self.abf.take3(4, T)
        v_tok = self.abf.take3(TT, 512)
        self.abf.push()
        wkT = self.abf.take3(8, 512)
        wvT = self.abf.take3(8, 512)
        P.need("dW1c", cast_tgt)
        tk = self.read_wt_stripes(scr["k"], wkT, D, 512)
        tv = self.read_wt_stripes(scr["v"], wvT, D, 512)
        P.phase()
        P.need("dT", tv)
        groups = []
        for pr in range(4):
            for tq in range(4):
                def mkk(e, p, pr=pr, tq=tq):
                    inst = None
                    for ic in range(FC):
                        inst = e.matmul(
                            p, wkT[:, ic, pr * 128:(pr + 1) * 128],
                            self.xn_chunk(ic)[:, tq * 512:(tq + 1) * 512],
                            start=(ic == 0), stop=(ic == FC - 1))
                    return inst
                def evk(e, p, pr=pr, tq=tq):
                    sl = slice(tq * 512, (tq + 1) * 512)
                    return e.tensor_scalar(k_fm[:, pr, sl], p,
                                           bkc[:, pr:pr + 1], None, ALU.add)
                groups.append((mkk, evk))
        self.mm_waves(groups, evac_engines=("vector",))
        groups = []
        for t in range(TT):
            def mkv(e, p, t=t):
                for ic in range(FC):
                    e.matmul(p, self.xn_chunk(ic)[:, t * 128:(t + 1) * 128],
                             wvT[:, ic, :], start=(ic == 0), stop=False)
                return e.matmul(p, self.ones_bf[0:1, :], bvrow[0:1, :],
                                start=False, stop=True)
            def evv(e, p, t=t):
                return self._evac(e, v_tok[:, t, :], p)
            groups.append((mkv, evv))
        self.mm_waves(groups)
        self.abf.pop()
        O_q = self.abf.take3(32, 512)
        wqbuf = [self.abf.take3(8, 128), self.abf.take3(8, 128)]
        q_dup = self.abf.take3(2, 512)
        Pslab = [self.abf.take3(8, 256), self.abf.take3(8, 256)]
        PTslab = self.abf.take3(8, 256)
        wobuf = [self.abf.take(512) for _ in range(4)]
        st = self.stat
        NJ = 32

        def issue_wq(j, w):
            return P.dma("sync", f"dWq{w}", lambda e, j=j, w=w: e.dma_start(
                out=wqbuf[w],
                in_=scr["q"].rearrange("(c p) n -> p c n", p=128)[
                    :, :, j * 128:(j + 1) * 128]))

        for quar in range(4):
            wqt = {0: issue_wq(0, 0)}
            P.phase()
            for itj in range(NJ + 4):
                jA, jB, jC, jD = itj, itj - 1, itj - 2, itj - 3
                if jA < NJ:
                    P.need(f"dWq{jA % 2}", wqt[jA])
                    def mkq(e, jA=jA, quar=quar):
                        inst = None
                        for ic in range(FC):
                            inst = e.matmul(
                                self.pb[5], wqbuf[jA % 2][:, ic, :],
                                self.xn_chunk(ic)[:, quar * 512:(quar + 1) * 512],
                                start=(ic == 0), stop=(ic == FC - 1))
                        return inst
                    P.op("tensor", mkq)
                    if jA + 1 < NJ:
                        wqt[jA + 1] = issue_wq(jA + 1, (jA + 1) % 2)
                if 0 <= jB < NJ:
                    P.op("vector", lambda e, jB=jB: e.tensor_tensor(
                        st[:, 8:16], st[:, 0:8], negsinkrep[:, jB, :],
                        ALU.min))
                if 0 <= jC < NJ:
                    sp = 16 + (jC % 2) * 16
                    P.op("vector", lambda e, sp=sp: e.tensor_add(
                        st[:, sp:sp + 8], st[:, sp:sp + 8],
                        st[:, sp + 8:sp + 16]))
                if 0 <= jD < NJ:
                    for u in range(8):
                        P.op("vector", lambda e, u=u: e.tensor_copy(
                            PTslab[:, u, 0:128],
                            self.pt[0][:, u * 128:(u + 1) * 128]))
                        P.op("scalar", lambda e, u=u: e.copy(
                            PTslab[:, u, 128:256],
                            self.pt[1][:, u * 128:(u + 1) * 128]))
                P.phase()
                if jA < NJ:
                    for h2 in range(2):
                        for half in range(2):
                            P.op("vector", lambda e, jA=jA, h2=h2, half=half:
                                 e.tensor_scalar(
                                     q_dup[half * 64:half * 64 + 64, h2, :],
                                     self.pb[5][h2 * 64:h2 * 64 + 64, :],
                                     0.125, bq8[h2 * 64:h2 * 64 + 64,
                                                jA:jA + 1],
                                     ALU.mult, ALU.add))
                if 0 <= jB < NJ:
                    sp = 16 + (jB % 2) * 16
                    for u in range(8):
                        P.op("scalar", lambda e, u=u, jB=jB, sp=sp:
                             e.activation(
                                 Pslab[jB % 2][:, u, :],
                                 self.pbig.rearrange(
                                     "p (u n) -> p u n", u=8)[:, u, :],
                                 AF.Exp, bias=st[:, 8 + u:9 + u],
                                 accum_out=st[:, sp + u:sp + u + 1]))
                if 0 <= jC < NJ:
                    sp = 16 + (jC % 2) * 16
                    P.op("vector", lambda e, sp=sp: e.reciprocal(
                        st[:, sp:sp + 8], st[:, sp:sp + 8]))
                if 0 <= jD < NJ:
                    g = jD // 4
                    for u in range(8):
                        qtl, h2 = u // 2, u % 2
                        qt = quar * 4 + qtl
                        def mkpv(e, u=u, qtl=qtl, h2=h2, qt=qt, g=g):
                            inst = None
                            nt_ = 1 if qt == 0 else 2
                            for i in range(nt_):
                                ktv = qt if nt_ == 1 else qt - 1 + i
                                inst = e.matmul(
                                    self.pb[4][h2 * 64:h2 * 64 + 64,
                                               qtl * 128:(qtl + 1) * 128],
                                    v_tok[:, ktv, g * 64:(g + 1) * 64],
                                    PTslab[:, u, i * 128:(i + 1) * 128],
                                    start=(i == 0), stop=(i == nt_ - 1))
                            return inst
                        P.op("tensor", mkpv)
                P.phase()
                if jA < NJ:
                    g = jA // 4
                    kp = (g % 2) * 64
                    pr = g // 2
                    for qtl in range(4):
                        qt = quar * 4 + qtl
                        for h2 in range(2):
                            def mks(e, qtl=qtl, qt=qt, h2=h2, pr=pr, kp=kp):
                                reg = self.pb[qtl][:, h2 * 256:(h2 + 1) * 256]
                                if qt == 0:
                                    e.matmul(reg[:, 128:256], self.ident_bf,
                                             self.mask256[:, 128:256],
                                             start=True, stop=True)
                                    e.matmul(reg[:, 0:128], self.ident_bf,
                                             self.mask256[:, 0:128],
                                             start=True, stop=False)
                                    return e.matmul(
                                        reg[:, 0:128],
                                        q_dup[kp:kp + 64, h2,
                                              qtl * 128:(qtl + 1) * 128],
                                        k_fm[kp:kp + 64, pr, 0:128],
                                        start=False, stop=True)
                                e.matmul(reg, self.ident_bf, self.mask_w,
                                         start=True, stop=False)
                                return e.matmul(
                                    reg,
                                    q_dup[kp:kp + 64, h2,
                                          qtl * 128:(qtl + 1) * 128],
                                    k_fm[kp:kp + 64, pr,
                                         (qt - 1) * 128:(qt + 1) * 128],
                                    start=False, stop=True)
                            P.op("tensor", mks)
                if 0 <= jB < NJ:
                    P.op("vector", lambda e, jB=jB: e.tensor_tensor(
                        st[:, 48:56], st[:, 8:16], negsinkrep[:, jB, :],
                        ALU.subtract))
                if 0 <= jC < NJ:
                    sp = 16 + (jC % 2) * 16
                    for u in range(8):
                        P.op("vector", lambda e, u=u, jC=jC, sp=sp:
                             e.tensor_scalar(
                                 Pslab[jC % 2][:, u, :], Pslab[jC % 2][:, u, :],
                                 st[:, sp + u:sp + u + 1], None, ALU.mult))
                if 0 <= jD < NJ:
                    P.op("vector", lambda e, jD=jD: e.tensor_copy(
                        O_q[:, jD, :], self.pb[4][:, 0:512]))
                P.phase()
                if jA < NJ:
                    P.op("vector", lambda e: e.reduce_max(
                        st[:, 0:8],
                        self.pbig.rearrange("p (u n) -> p u n", u=8),
                        axis=AX.X, negate=True))
                if 0 <= jB < NJ:
                    sp = 16 + (jB % 2) * 16
                    P.op("scalar", lambda e, sp=sp: e.activation(
                        st[:, sp + 8:sp + 16], st[:, 48:56], AF.Exp))
                if 0 <= jC < NJ:
                    for u in range(8):
                        for kt in range(2):
                            P.op("tensor", lambda e, u=u, kt=kt, jC=jC:
                                 e.transpose(
                                     self.pt[kt][:, u * 128:(u + 1) * 128],
                                     Pslab[jC % 2][:, u,
                                                   kt * 128:(kt + 1) * 128],
                                     self.ident_bf))
                P.phase()
            for dh in range(2):
                def issue_wo1(j, dh=dh):
                    w = j % 4
                    return P.dma("sync", f"dWo1{w}", lambda e, j=j, w=w, dh=dh:
                                 e.dma_start(
                        out=wobuf[w], in_=scr["o"][j * 128:(j + 1) * 128,
                                                   dh * 512:(dh + 1) * 512]))
                wot = {0: issue_wo1(0), 1: issue_wo1(1)}
                P.phase()
                for j0 in range(0, 32, 2):
                    P.need(f"dWo1{j0 % 4}", wot[j0])
                    P.need(f"dWo1{(j0 + 1) % 4}", wot[j0 + 1])
                    def mko(e, j0=j0):
                        inst = None
                        for j in (j0, j0 + 1):
                            for tl in range(4):
                                inst = e.matmul(
                                    self.pb[tl],
                                    O_q[:, j, tl * 128:(tl + 1) * 128],
                                    wobuf[j % 4], start=(j == 0),
                                    stop=(j == 31))
                        return inst
                    P.op("tensor", mko)
                    for nx in (j0 + 2, j0 + 3):
                        if nx < 32:
                            wot[nx] = issue_wo1(nx)
                    P.phase()
                for tl in range(4):
                    P.op("vector", lambda e, tl=tl, dh=dh: e.tensor_tensor(
                        self.pb[tl], self.pb[tl],
                        bobc[:, dh * 512:(dh + 1) * 512], ALU.add))
                P.phase()
                for tl in range(4):
                    t = quar * 4 + tl
                    xs = self.X[:, t * D + dh * 512: t * D + (dh + 1) * 512]
                    P.op("vector", lambda e, xs=xs, tl=tl: e.tensor_tensor(
                        xs, xs, self.pb[tl], ALU.add))
                P.phase()
        self.af32.pop()
        self.abf.pop()


    def sl6_prep(self, sim_m, gates, wq, wk, wv, wo, scr):
        P = self.P
        simn = self.af32.take(128).rearrange("p (a b) -> p a b", a=8)
        sgbc = self.af32.take(16)
        for nm in ("q", "k", "v", "o"):
            src = {"q": wq, "k": wk, "v": wv, "o": wo}[nm]
            P.dma("gpsimd", "dW6", lambda e, s=src, d=scr[nm]: e.dma_start(
                out=d, in_=s))
        self.cast6_tgt = P.dma_count("dW6")
        self.abf.push()
        sq = self.abf.take(256).bitcast(F32)
        row = self.abf.take(64).bitcast(F32)
        tg = P.dma("sync", "dS6", lambda e: e.dma_start(
            out=simn, in_=sim_m.rearrange("(c p) e -> p c e", p=128)))
        tg = P.dma("sync", "dS6", lambda e: e.dma_start(
            out=row[0:1, 16:32], in_=gates.rearrange("(o e) -> o e", o=1)))
        P.phase()
        P.need("dS6", tg)
        P.need("dC", self.const_tgt)
        P.op("scalar", lambda e: e.activation(
            sq, simn.rearrange("p a b -> p (a b)"), AF.Square))
        P.phase()
        P.op("tensor", lambda e: e.matmul(
            self.pb[0][0:1, 0:128], self.onec_f32, sq, start=True, stop=True))
        P.phase()
        P.op("vector", lambda e: e.reduce_sum(
            row[0:1, 0:16],
            self.pb[0][0:1, 0:128].rearrange("p (c e) -> p e c", c=8), axis=AX.X))
        P.phase()
        P.op("vector", lambda e: e.tensor_scalar_add(row[0:1, 0:16],
                                                     row[0:1, 0:16], 1e-12))
        P.phase()
        P.op("scalar", lambda e: e.activation(row[0:1, 0:16], row[0:1, 0:16],
                                              AF.Sqrt))
        P.phase()
        P.op("vector", lambda e: e.reciprocal(row[0:1, 0:16], row[0:1, 0:16]))
        P.op("scalar", lambda e: e.activation(row[0:1, 16:32], row[0:1, 16:32],
                                              AF.Sigmoid))
        P.phase()
        P.op("tensor", lambda e: e.matmul(
            self.pb[0][:, 0:16], self.ones_f32, row[0:1, 0:16],
            start=True, stop=True))
        P.op("tensor", lambda e: e.matmul(
            self.pb[1][:, 0:16], self.ones_f32, row[0:1, 16:32],
            start=True, stop=True))
        P.phase()
        P.op("vector", lambda e: e.tensor_copy(sgbc, self.pb[1][:, 0:16]))
        P.op("scalar", lambda e: e.copy(sq[:, 0:16], self.pb[0][:, 0:16]))
        P.phase()
        for c in range(8):
            P.op("vector", lambda e, c=c: e.tensor_mul(
                simn[:, c, :], simn[:, c, :], sq[:, 0:16]))
        P.phase()
        self.abf.pop()
        return simn, sgbc

    def sl6(self, simn, sgbc, scr, first=False):
        P = self.P
        needs = [("dX", self.x_tgt), ("dC", self.const_tgt)] if first else []
        self.emit_ln(needs=needs)
        st = self.stat
        var = st[:, 48:64]
        self.abf.push()
        self.af32.push()
        rl2 = self.af32.take(16)
        rw = self.af32.take(256).rearrange("p (a b) -> p a b", a=16)
        lg = self.af32.take(16)
        cols = self.af32.take(8)
        xnT = self.abf.take(2048).bitcast(F32).rearrange(
            "p (a b) -> p a b", a=8)
        tmp32 = self.abf.take(2048).bitcast(F32)
        ebuf = self.af32.take(16)
        gat = self.af32.take(16)
        msk = self.af32.take(16)
        fb = self.af32.take(16)
        P.op("vector", lambda e: e.reciprocal(st[:, 0:16], var))
        P.phase()
        P.op("vector", lambda e: e.tensor_scalar(
            st[:, 0:16], st[:, 0:16], -1024.0 * LN_EPS, 1024.0 + 1e-12,
            ALU.mult, ALU.add))
        P.phase()
        P.op("scalar", lambda e: e.activation(st[:, 0:16], st[:, 0:16], AF.Sqrt))
        P.phase()
        P.op("vector", lambda e: e.reciprocal(rl2, st[:, 0:16]))
        P.phase()
        for t in range(TT):
            P.op("vector", lambda e, t=t: e.tensor_scalar(
                tmp32, self.X[:, t * D:(t + 1) * D],
                self.ln_b[:, t:t + 1], self.ln_rstd[:, t:t + 1],
                ALU.add, ALU.mult))
            P.phase()
            items = [(tmp32[:, fc * 128:(fc + 1) * 128], xnT[:, fc, :])
                     for fc in range(FC)]
            self.transpose_waves(items, f32=True)
            def mklg(e, t=t):
                inst = None
                for ic in range(FC):
                    inst = e.matmul(self.pb[0][:, 0:16], xnT[:, ic, :],
                                    simn[:, ic, :], start=(ic == 0),
                                    stop=(ic == FC - 1))
                return inst
            P.op("tensor", mklg)
            P.phase()
            P.op("vector", lambda e, t=t: e.scalar_tensor_tensor(
                lg, self.pb[0][:, 0:16], rl2[:, t:t + 1], sgbc,
                ALU.mult, ALU.subtract))
            P.phase()
            P.op("vector", lambda e: e.reduce_max(cols[:, 0:1], lg, axis=AX.X))
            P.op("vector", lambda e: e.tensor_scalar_max(gat, lg, 0.0))
            P.phase()
            P.op("vector", lambda e: e.tensor_scalar(
                ebuf, lg, cols[:, 0:1], None, ALU.is_ge))
            P.op("vector", lambda e: e.tensor_single_scalar(
                cols[:, 2:3], cols[:, 0:1], 0.0, ALU.is_le))
            P.op("scalar", lambda e: e.activation(cols[:, 3:4], cols[:, 0:1],
                                                  AF.Relu))
            P.phase()
            P.op("vector", lambda e: e.scalar_tensor_tensor(
                ebuf, ebuf, -1e30, lg, ALU.mult, ALU.add))
            P.op("vector", lambda e: e.tensor_scalar_mul(cols[:, 4:5],
                                                         cols[:, 3:4], -1.0))
            P.phase()
            P.op("vector", lambda e: e.reduce_max(cols[:, 1:2], ebuf, axis=AX.X))
            P.phase()
            P.op("vector", lambda e: e.tensor_scalar(
                fb, lg, cols[:, 1:2], None, ALU.is_ge))
            P.op("vector", lambda e: e.tensor_single_scalar(msk, lg, 0.0,
                                                            ALU.is_gt))
            P.phase()
            P.op("vector", lambda e: e.scalar_tensor_tensor(
                msk, fb, cols[:, 2:3], msk, ALU.mult, ALU.add))
            P.phase()
            P.op("vector", lambda e: e.tensor_single_scalar(msk, msk, 0.0,
                                                            ALU.is_gt))
            P.phase()
            P.op("scalar", lambda e: e.activation(ebuf, gat, AF.Exp,
                                                  bias=cols[:, 4:5]))
            P.phase()
            P.op("vector", lambda e: e.tensor_mul(ebuf, ebuf, msk))
            P.phase()
            P.op("vector", lambda e: e.reduce_sum(cols[:, 5:6], ebuf, axis=AX.X))
            P.phase()
            P.op("vector", lambda e: e.reciprocal(cols[:, 5:6], cols[:, 5:6]))
            P.phase()
            P.op("vector", lambda e, t=t: e.tensor_scalar(
                rw[:, t, :], ebuf, cols[:, 5:6], None, ALU.mult))
            P.phase()
        wbuf = [self.abf.take3(8, 128), self.abf.take3(8, 128)]
        acc32 = self.abf.take(1024).bitcast(F32).rearrange(
            "p (a b) -> p a b", a=4)
        q_tok = self.abf.take3(TT, 128)
        k_tok = self.abf.take3(TT, 128)
        v_tok = self.abf.take3(TT, 128)
        q_fm = self.abf.take(T)
        k_fm = self.abf.take(T)
        ao_fm = self.abf.take3(TT, 128)
        P_row = self.abf.take(T)
        P_T = self.abf.take3(TT, 128)
        wobuf = [self.abf.take(512), self.abf.take(512)]
        P.phase()
        P.need("dW6", self.cast6_tgt)
        for proj, dst in (("q", q_tok), ("k", k_tok), ("v", v_tok)):
            s = scr[proj]
            def issue_w(e_, w, s=s):
                return P.dma("sync", f"dW6{w}", lambda e, e_=e_, w=w, s=s: e.dma_start(
                    out=wbuf[w], in_=s.rearrange("(E c p) n -> E p c n", E=16, p=128)[e_]))
            for quar in range(4):
                wtgt = {0: issue_w(0, 0)}
                P.phase()
                for e_ in range(17):
                    if e_ < 16:
                        w = e_ % 2
                        P.need(f"dW6{w}", wtgt[e_])
                        pbk = 4 + (e_ % 2)
                        def mkp(e, quar=quar, w=w, pbk=pbk):
                            inst = None
                            for tl in range(4):
                                t = quar * 4 + tl
                                for ic in range(FC):
                                    inst = e.matmul(
                                        self.pb[pbk][:, tl * 128:(tl + 1) * 128],
                                        self.xn_chunk(ic)[:, t * 128:(t + 1) * 128],
                                        wbuf[w][:, ic, :], start=(ic == 0),
                                        stop=(ic == FC - 1))
                            return inst
                        P.op("tensor", mkp)
                        if e_ + 1 < 16:
                            wtgt[e_ + 1] = issue_w(e_ + 1, (e_ + 1) % 2)
                    if e_ >= 1:
                        em = e_ - 1
                        pbp = 4 + (em % 2)
                        for tl in range(4):
                            t = quar * 4 + tl
                            if em == 0:
                                P.op("vector", lambda e, t=t, tl=tl, pbp=pbp:
                                     e.tensor_scalar(
                                         acc32[:, tl, :],
                                         self.pb[pbp][:, tl * 128:(tl + 1) * 128],
                                         rw[:, t, 0:1], None, ALU.mult))
                            else:
                                P.op("vector", lambda e, t=t, tl=tl, em=em, pbp=pbp:
                                     e.scalar_tensor_tensor(
                                         acc32[:, tl, :],
                                         self.pb[pbp][:, tl * 128:(tl + 1) * 128],
                                         rw[:, t, em:em + 1], acc32[:, tl, :],
                                         ALU.mult, ALU.add))
                    P.phase()
                csc = 1.0 / np.sqrt(128.0) if proj == "q" else 1.0
                for tl in range(4):
                    t = quar * 4 + tl
                    P.op("scalar", lambda e, t=t, tl=tl, dst=dst, csc=csc:
                         e.activation(dst[:, t, :], acc32[:, tl, :], AF.Copy,
                                      scale=csc))
                P.phase()
        items = [(q_tok[:, t, :], q_fm[:, t * 128:(t + 1) * 128]) for t in range(TT)]
        items += [(k_tok[:, t, :], k_fm[:, t * 128:(t + 1) * 128]) for t in range(TT)]
        self.transpose_waves(items)
        sc = 1.0 / np.sqrt(128.0)
        for qt in range(TT):
            ksp = (qt + 1) * 128
            nb = (ksp + 511) // 512
            dkb = qt // 4
            doff = (qt % 4) * 128
            for kb in range(nb):
                N = min(512, ksp - kb * 512)
                def mks(e, kb=kb, N=N, qt=qt, doff=doff, diag_=(kb == dkb)):
                    if diag_:
                        e.matmul(self.pb[kb][:, 0:N], self.ident_bf,
                                 self.maskpad[:, 384 - doff:384 - doff + N],
                                 start=True, stop=False)
                    return e.matmul(
                        self.pb[kb][:, 0:N], q_fm[:, qt * 128:(qt + 1) * 128],
                        k_fm[:, kb * 512:kb * 512 + N],
                        start=(not diag_), stop=True)
                P.op("tensor", mks)
            P.phase()
            P.op("vector", lambda e, ksp=ksp: e.reduce_max(
                st[:, 9:10], self.pbig[:, 0:ksp], axis=AX.X, negate=True))
            P.phase()
            P.op("scalar", lambda e, ksp=ksp: e.activation(
                P_row[:, 0:ksp], self.pbig[:, 0:ksp], AF.Exp,
                bias=st[:, 9:10], scale=1.0, accum_out=st[:, 24:25]))
            P.phase()
            P.op("vector", lambda e: e.reciprocal(st[:, 25:26], st[:, 24:25]))
            P.phase()
            P.op("vector", lambda e, ksp=ksp: e.tensor_scalar(
                P_row[:, 0:ksp], P_row[:, 0:ksp], st[:, 25:26], None, ALU.mult))
            P.phase()
            items = [(P_row[:, kt * 128:(kt + 1) * 128], P_T[:, kt, :])
                     for kt in range(qt + 1)]
            self.transpose_waves(items)
            def mkpv(e, qt=qt):
                inst = None
                for kt in range(qt + 1):
                    inst = e.matmul(self.pb[4][:, 0:128], v_tok[:, kt, :],
                                    P_T[:, kt, :], start=(kt == 0),
                                    stop=(kt == qt))
                return inst
            P.op("tensor", mkpv)
            P.phase()
            P.op("vector", lambda e, qt=qt: e.tensor_copy(
                ao_fm[:, qt, :], self.pb[4][:, 0:128]))
            P.phase()
        def issue_wo(e_, ch, w):
            return P.dma("sync", f"dWo{w}", lambda e, e_=e_, ch=ch, w=w: e.dma_start(
                out=wobuf[w],
                in_=scr["o"][e_ * 128:(e_ + 1) * 128, ch * 512:(ch + 1) * 512]))
        for quar in range(4):
            for ch in range(2):
                wtgt = {0: issue_wo(0, ch, 0)}
                P.phase()
                for st_ in range(33):
                    if st_ < 32:
                        e_, half = st_ // 2, st_ % 2
                        if half == 0:
                            P.need(f"dWo{e_ % 2}", wtgt[e_])
                        b0 = 0 if st_ % 2 == 0 else 2
                        def mkoe(e, quar=quar, st_=st_, b0=b0):
                            e_, half = st_ // 2, st_ % 2
                            inst = None
                            for i in range(2):
                                t = quar * 4 + half * 2 + i
                                inst = e.matmul(self.pb[b0 + i], ao_fm[:, t, :],
                                                wobuf[e_ % 2], start=True,
                                                stop=True)
                            return inst
                        P.op("tensor", mkoe)
                        if half == 1 and e_ + 1 < 16:
                            wtgt[e_ + 1] = issue_wo(e_ + 1, ch, (e_ + 1) % 2)
                    if st_ >= 1:
                        sm = st_ - 1
                        em, hm = sm // 2, sm % 2
                        b0 = 0 if sm % 2 == 0 else 2
                        for i in range(2):
                            t = quar * 4 + hm * 2 + i
                            xs = self.X[:, t * D + ch * 512:
                                        t * D + (ch + 1) * 512]
                            P.op("vector", lambda e, xs=xs, i=i, b0=b0, em=em,
                                 t=t: e.scalar_tensor_tensor(
                                     xs, self.pb[b0 + i], rw[:, t, em:em + 1],
                                     xs, ALU.mult, ALU.add))
                    P.phase()
        self.af32.pop()
        self.abf.pop()


    def _elu1_inplace(self, units, slots):
        P = self.P
        nu = len(units)
        for i in range(nu + 2):
            if i < nu:
                x = units[i]
                s = slots[i % 3]
                n = x.shape[-1]
                P.op("vector", lambda e, x=x, s=s, n=n: e.tensor_scalar_min(
                    s[:, 0:n], x, 0.0))
            if 0 <= i - 1 < nu:
                x = units[i - 1]
                s = slots[(i - 1) % 3]
                n = x.shape[-1]
                P.op("scalar", lambda e, s=s, n=n: e.activation(
                    s[:, 0:n], s[:, 0:n], AF.Exp))
            if 0 <= i - 2 < nu:
                x = units[i - 2]
                s = slots[(i - 2) % 3]
                n = x.shape[-1]
                P.op("vector", lambda e, x=x, s=s, n=n: e.scalar_tensor_tensor(
                    x, x, 0.0, s[:, 0:n], ALU.max, ALU.add))
            P.phase()

    def sl5(self, wqkv, wout, gamma, beta, scr_qkv, scr_out, first=False, upto="full"):
        P = self.P
        P.dma("gpsimd", "dW5", lambda e: e.dma_start(out=scr_qkv, in_=wqkv))
        P.dma("gpsimd", "dW5", lambda e: e.dma_start(out=scr_out, in_=wout))
        cast_tgt = P.dma_count("dW5")
        P.phase()
        needs = [("dX", self.x_tgt), ("dC", self.const_tgt)] if first else []
        self.emit_ln(needs=needs, upto="lncol")
        st = self.stat
        var = st[:, 48:64]
        s2 = self.af32.take(16)
        P.op("vector", lambda e: e.tensor_mul(st[:, 0:16], self.ln_rstd,
                                              self.ln_rstd))
        P.phase()
        P.op("vector", lambda e: e.scalar_tensor_tensor(
            st[:, 0:16], st[:, 0:16], 1.0, var, ALU.mult, ALU.mult))
        P.phase()
        P.op("vector", lambda e: e.tensor_scalar_add(st[:, 0:16], st[:, 0:16],
                                                     LN_EPS))
        P.phase()
        P.op("scalar", lambda e: e.activation(st[:, 16:32], st[:, 0:16], AF.Sqrt))
        P.phase()
        P.op("vector", lambda e: e.reciprocal(st[:, 16:32], st[:, 16:32]))
        P.phase()
        P.op("vector", lambda e: e.tensor_mul(s2, self.ln_rstd, st[:, 16:32]))
        P.phase()
        self.abf.push()
        gbr = self.abf.take(4096).bitcast(F32)
        gbc = self.abf.take(8192).bitcast(F32)
        tmp2 = self.abf.take(4096).bitcast(F32)
        tgb = P.dma("sync", "dG5", lambda e: e.dma_start(
            out=gbr[0:1, 0:1024], in_=gamma.rearrange("(o d) -> o d", o=1)))
        tgb = P.dma("sync", "dG5", lambda e: e.dma_start(
            out=gbr[0:1, 1024:2048], in_=beta.rearrange("(o d) -> o d", o=1)))
        P.phase()
        P.need("dG5", tgb)
        for j in range(4):
            P.op("tensor", lambda e, j=j: e.matmul(
                self.pb[j][:, 0:512], self.ones_f32[0:1, :],
                gbr[0:1, j * 512:(j + 1) * 512], start=True, stop=True))
        P.phase()
        for j in range(4):
            P.op("vector" if j % 2 == 0 else "scalar",
                 lambda e, j=j: e.tensor_copy(gbc[:, j * 512:(j + 1) * 512],
                                              self.pb[j][:, 0:512])
                 if hasattr(e, "tensor_copy") else e.copy(
                     gbc[:, j * 512:(j + 1) * 512], self.pb[j][:, 0:512]))
        P.phase()
        gamma_bc = gbc[:, 0:1024]
        beta_bc = gbc[:, 1024:2048]
        for g in range(TT // 2):
            for i in range(2):
                t = 2 * g + i
                P.op("vector",
                     lambda e, t=t, i=i: e.tensor_scalar(
                         tmp2[:, i * 1024:(i + 1) * 1024],
                         self.X[:, t * D:(t + 1) * D],
                         self.ln_b[:, t:t + 1], s2[:, t:t + 1],
                         ALU.add, ALU.mult))
            P.phase()
            for i in range(2):
                P.op("vector",
                     lambda e, i=i: e.tensor_mul(
                         tmp2[:, i * 1024:(i + 1) * 1024],
                         tmp2[:, i * 1024:(i + 1) * 1024], gamma_bc))
            P.phase()
            for i in range(2):
                P.op("vector",
                     lambda e, i=i: e.tensor_add(
                         self.xtbuf[:, i, :],
                         tmp2[:, i * 1024:(i + 1) * 1024], beta_bc))
            P.phase()
            items = []
            for i in range(2):
                t = 2 * g + i
                for fc in range(FC):
                    items.append((self.xtbuf[:, i, fc * 128:(fc + 1) * 128],
                                  self.xn_fm[:, fc * T + t * 128:fc * T + (t + 1) * 128]))
            self.transpose_waves(items)
        self.abf.pop()
        if upto == "xn2":
            return
        self.abf.push()
        wbuf = self.abf.take3(8, 512)
        R1 = self.abf.take3(TT, D)
        R2 = self.abf.take3(TT, D)
        kv_sb = self.abf.take3(16, 64)
        ksum_bf = self.abf.take(16)
        elu3 = self.abf.take(1024)
        dcol = self.af32.take(16)
        q_fm = R1.rearrange("p a b -> p (a b)").rearrange(
            "p (a b) -> p a b", a=FC)
        P.phase()
        P.need("dW5", cast_tgt)
        for blk, dstR, kind in (("k", R1, "tok"), ("v", R2, "tok")):
            off = 1024 if blk == "k" else 2048
            for oh in range(2):
                tw = P.dma("sync", "dWB5", lambda e, off=off, oh=oh: e.dma_start(
                    out=wbuf, in_=scr_qkv.rearrange("(c p) n -> p c n", p=128)[
                        :, :, off + oh * 512: off + (oh + 1) * 512]))
                P.phase()
                P.need("dWB5", tw)
                groups = []
                for t in range(TT):
                    def mkp(e, p, t=t, oh=oh):
                        inst = None
                        for ic in range(FC):
                            inst = e.matmul(
                                p, self.xn_chunk(ic)[:, t * 128:(t + 1) * 128],
                                wbuf[:, ic, :], start=(ic == 0),
                                stop=(ic == FC - 1))
                        return inst
                    def evp(e, p, t=t, oh=oh, dstR=dstR):
                        return self._evac(e, dstR[:, t, oh * 512:(oh + 1) * 512], p)
                    groups.append((mkp, evp))
                self.mm_waves(groups)
        if upto == "kv":
            return
        self._elu1_inplace([R1[:, t, :] for t in range(TT)],
                           [self.xtbuf[:, 0, :], self.xtbuf[:, 1, :], elu3])
        if upto == "eluk":
            return
        for h in range(16):
            def mkkv(e, h=h):
                inst = None
                for kt in range(TT):
                    inst = e.matmul(self.pb[0][0:64, 0:64],
                                    R1[:, kt, h * 64:(h + 1) * 64],
                                    R2[:, kt, h * 64:(h + 1) * 64],
                                    start=(kt == 0), stop=(kt == TT - 1))
                return inst
            def mkks(e, h=h):
                inst = None
                for kt in range(TT):
                    inst = e.matmul(self.pb[1][0:64, 0:1],
                                    R1[:, kt, h * 64:(h + 1) * 64],
                                    self.ones_bf[:, 0:1],
                                    start=(kt == 0), stop=(kt == TT - 1))
                return inst
            P.op("tensor", mkkv)
            P.op("tensor", mkks)
            P.phase()
            P.op("vector", lambda e, h=h: e.tensor_copy(
                kv_sb[0:64, h, :], self.pb[0][0:64, 0:64]))
            P.op("vector", lambda e, h=h: e.tensor_copy(
                kv_sb[64:128, h, :], self.pb[0][0:64, 0:64]))
            P.op("scalar", lambda e, h=h: e.copy(
                ksum_bf[0:64, h:h + 1], self.pb[1][0:64, 0:1]))
            P.op("scalar", lambda e, h=h: e.copy(
                ksum_bf[64:128, h:h + 1], self.pb[1][0:64, 0:1]))
            P.phase()
        if upto == "kvsum":
            return
        for oh in range(2):
            tw = P.dma("sync", "dWB5", lambda e, oh=oh: e.dma_start(
                out=wbuf, in_=scr_qkv.rearrange("(c p) n -> p c n", p=128)[
                    :, :, oh * 512:(oh + 1) * 512]))
            P.phase()
            P.need("dWB5", tw)
            groups = []
            for ocl in range(4):
                oc = oh * 4 + ocl
                for tq in range(4):
                    def mkq(e, p, ocl=ocl, tq=tq):
                        inst = None
                        for ic in range(FC):
                            inst = e.matmul(
                                p, wbuf[:, ic, ocl * 128:(ocl + 1) * 128],
                                self.xn_chunk(ic)[:, tq * 512:(tq + 1) * 512],
                                start=(ic == 0), stop=(ic == FC - 1))
                        return inst
                    def evq(e, p, oc=oc, tq=tq):
                        return self._evac(e, q_fm[:, oc, tq * 512:(tq + 1) * 512], p)
                    groups.append((mkq, evq))
            self.mm_waves(groups)
        units = []
        for oc in range(FC):
            units.append(q_fm[:, oc, 0:1024])
            units.append(q_fm[:, oc, 1024:2048])
        self._elu1_inplace(units, [self.xtbuf[:, 0, :], self.xtbuf[:, 1, :], elu3])
        if upto == "qproj":
            return
        for t in range(TT):
            for h in range(16):
                hb = (h % 2) * 64
                db = 2 if h % 2 == 0 else 5
                P.op("tensor", lambda e, h=h, hb=hb, t=t, db=db: e.matmul(
                    self.pb[db][:, h // 2:h // 2 + 1],
                    q_fm[hb:hb + 64, h // 2, t * 128:(t + 1) * 128],
                    ksum_bf[hb:hb + 64, h:h + 1], start=True, stop=True))
                qb = 3 + h % 2
                P.op("tensor", lambda e, h=h, hb=hb, t=t, qb=qb: e.matmul(
                    self.pb[qb][:, (h // 2) * 64:(h // 2 + 1) * 64],
                    q_fm[hb:hb + 64, h // 2, t * 128:(t + 1) * 128],
                    kv_sb[hb:hb + 64, h, :], start=True, stop=True))
            P.phase()
            P.op("vector", lambda e: e.tensor_scalar_add(
                dcol[:, 0:8], self.pb[2][:, 0:8], 1e-6))
            P.op("vector", lambda e: e.tensor_scalar_add(
                dcol[:, 8:16], self.pb[5][:, 0:8], 1e-6))
            P.phase()
            P.op("vector", lambda e: e.reciprocal(dcol, dcol))
            P.phase()
            for h in range(16):
                qb = 3 + h % 2
                dc = (h % 2) * 8 + h // 2
                P.op("vector", lambda e, h=h, t=t, qb=qb, dc=dc: e.tensor_scalar(
                    R2[:, t, h * 64:(h + 1) * 64],
                    self.pb[qb][:, (h // 2) * 64:(h // 2 + 1) * 64],
                    dcol[:, dc:dc + 1], None, ALU.mult))
            P.phase()
        if upto == "attn":
            return
        for g in range(TT // 2):
            items = []
            for i in range(2):
                t = 2 * g + i
                for fc in range(FC):
                    items.append((R2[:, t, fc * 128:(fc + 1) * 128],
                                  self.xn_fm[:, fc * T + t * 128:fc * T + (t + 1) * 128]))
            self.transpose_waves(items)
        tmpr = self.abf.take(2048).bitcast(F32)
        for oh in range(2):
            tw = P.dma("sync", "dWB5", lambda e, oh=oh: e.dma_start(
                out=wbuf, in_=scr_out.rearrange("(c p) n -> p c n", p=128)[
                    :, :, oh * 512:(oh + 1) * 512]))
            P.phase()
            P.need("dWB5", tw)
            for t in range(TT):
                bk = t % 2
                def mkо(e, t=t, bk=bk):
                    inst = None
                    for ic in range(FC):
                        inst = e.matmul(
                            self.pb[bk], self.xn_chunk(ic)[:, t * 128:(t + 1) * 128],
                            wbuf[:, ic, :], start=(ic == 0), stop=(ic == FC - 1))
                    return inst
                P.op("tensor", mkо)
                P.op("vector", lambda e, t=t, oh=oh: e.tensor_scalar(
                    tmpr[:, (t % 2) * 512:(t % 2) * 512 + 512],
                    self.X[:, t * D + oh * 512:t * D + oh * 512 + 512],
                    self.ln_b[:, t:t + 1], self.ln_rstd[:, t:t + 1],
                    ALU.add, ALU.mult))
                P.phase()
                P.op("vector", lambda e, t=t, bk=bk: e.tensor_tensor(
                    tmpr[:, bk * 512:bk * 512 + 512],
                    tmpr[:, bk * 512:bk * 512 + 512], self.pb[bk], ALU.add))
                P.phase()
                xs = self.X[:, t * D + oh * 512:t * D + oh * 512 + 512]
                P.op("vector", lambda e, xs=xs, bk=bk: e.tensor_tensor(
                    xs, xs, tmpr[:, bk * 512:bk * 512 + 512], ALU.add))
                P.phase()
        self.abf.pop()


    def sl3(self, wq, wk, wv, wo, scr, first=False):
        P = self.P
        names = ["q", "k", "v", "o"]
        wts = [wq, wk, wv, wo]
        for i in range(4):
            cast_tgt = P.dma("gpsimd", "dW3",
                             lambda e, s=wts[i], d=scr[i]: e.dma_start(out=d, in_=s))
        P.phase()
        needs = [("dX", self.x_tgt), ("dC", self.const_tgt)] if first else []
        self.emit_ln(needs=needs)
        self.abf.push()
        wqt = self.abf.take3(8, D)
        k_fm = self.abf.take3(8, T)
        v_tok = self.abf.take3(TT, D)
        q_qt = self.abf.take3(8, 128)
        O_q = self.abf.take3(8, 512)
        pa = self.abf.take(4096)
        P_row = pa[:, 0:2048]
        P_rowB = self.xtbuf.rearrange("p a b -> p (a b)")
        P_T = pa[:, 2048:4096].rearrange("p (a b) -> p a b", a=16)
        wot = pa.rearrange("p (a b) -> p a b", a=8)
        st = self.stat

        P.need("dW3", P.dma_count("dW3"))
        tgt = self.read_wt_stripes(scr[1], wqt, D, D)
        P.phase()
        self._proj_fm(k_fm, wqt, tgt, scale=1.0)
        tgt = self.read_wt_stripes(scr[2], wqt, D, D)
        P.phase()
        self._proj_tok(v_tok, wqt, tgt)
        tgt = self.read_wt_stripes(scr[0], wqt, D, D)
        P.phase()
        P.need("dT", tgt)
        P.op("sync", lambda e: e.nop())
        P.phase()
        for quar in range(4):
            for qtl in range(4):
                qt = quar * 4 + qtl
                groups = []
                for oc in range(FC):
                    def mkq(e, p, oc=oc, qt=qt):
                        inst = None
                        for ic in range(FC):
                            inst = e.matmul(
                                p[:, 0:128], wqt[:, ic, oc * 128:(oc + 1) * 128],
                                self.xn_chunk(ic)[:, qt * 128:(qt + 1) * 128],
                                start=(ic == 0), stop=(ic == FC - 1))
                        return inst
                    def evq(e, p, oc=oc):
                        return self._evac(e, q_qt[:, oc, :], p[:, 0:128], 0.125)
                    groups.append((mkq, evq))
                self.mm_waves(groups)
                ksp = (qt + 1) * 128
                nb = (ksp + 511) // 512
                dkb = qt // 4
                doff = (qt % 4) * 128
                nt = qt + 1

                def prow(h):
                    return P_row if h % 2 == 0 else P_rowB

                for it in range(19):
                    hS, hT, hP, hO = it, it - 1, it - 2, it - 3
                    if hS < 16:
                        for kb in range(nb):
                            N = min(512, ksp - kb * 512)
                            def mks(e, kb=kb, N=N, pr=hS // 2,
                                    hb=(hS % 2) * 64, doff=doff,
                                    diag=(kb == dkb)):
                                if diag:
                                    e.matmul(self.pb[kb][:, 0:N], self.ident_bf,
                                             self.maskpad[:, 384 - doff:384 - doff + N],
                                             start=True, stop=False)
                                return e.matmul(
                                    self.pb[kb][:, 0:N],
                                    q_qt[hb:hb + 64, pr, :],
                                    k_fm[hb:hb + 64, pr, kb * 512:kb * 512 + N],
                                    start=(not diag), stop=True)
                            P.op("tensor", mks)
                    if 0 <= hP < 16 and nt > 8:
                        for kt in range(8, nt):
                            P.op("tensor", lambda e, kt=kt, hP=hP: e.transpose(
                                self.pt[1][:, (kt - 8) * 128:(kt - 7) * 128],
                                prow(hP)[:, kt * 128:(kt + 1) * 128],
                                self.ident_bf))
                    if 0 <= hT < 16:
                        P.op("vector", lambda e: e.reciprocal(st[:, 25:26],
                                                              st[:, 24:25]))
                    if 0 <= hP < 16:
                        for kt in range(min(8, nt)):
                            P.op("vector", lambda e, kt=kt: e.tensor_copy(
                                P_T[:, kt, :],
                                self.pt[0][:, kt * 128:(kt + 1) * 128]))
                    if 0 <= hO < 16 and hO % 2 == 1:
                        P.op("vector", lambda e, pr=hO // 2, qtl=qtl:
                             e.tensor_copy(
                                 O_q[:, pr, qtl * 128:(qtl + 1) * 128],
                                 self.pb[4][:, 0:128]))
                    P.phase()
                    if hS < 16:
                        P.op("vector", lambda e, ksp=ksp: e.reduce_max(
                            st[:, 9:10], self.pbig[:, 0:ksp], axis=AX.X,
                            negate=True))
                    if 0 <= hT < 16:
                        P.op("vector", lambda e, hT=hT, ksp=ksp: e.tensor_scalar(
                            prow(hT)[:, 0:ksp], prow(hT)[:, 0:ksp],
                            st[:, 25:26], None, ALU.mult))
                    if 0 <= hP < 16 and nt > 8:
                        for kt in range(8, nt):
                            P.op("scalar", lambda e, kt=kt: e.copy(
                                P_T[:, kt, :],
                                self.pt[1][:, (kt - 8) * 128:(kt - 7) * 128]))
                    P.phase()
                    if hS < 16:
                        P.op("scalar", lambda e, ksp=ksp, hS=hS: e.activation(
                            prow(hS)[:, 0:ksp], self.pbig[:, 0:ksp],
                            AF.Exp, bias=st[:, 9:10], scale=1.0,
                            accum_out=st[:, 24:25]))
                    if 0 <= hT < 16:
                        for kt in range(min(8, nt)):
                            P.op("tensor", lambda e, kt=kt, hT=hT: e.transpose(
                                self.pt[0][:, kt * 128:(kt + 1) * 128],
                                prow(hT)[:, kt * 128:(kt + 1) * 128],
                                self.ident_bf))
                    if 0 <= hP < 16:
                        def mkpv(e, qt=qt, hP=hP, hb=(hP % 2) * 64):
                            inst = None
                            for kt in range(qt + 1):
                                inst = e.matmul(
                                    self.pb[4][hb:hb + 64, 0:128],
                                    v_tok[:, kt, hP * 64:(hP + 1) * 64],
                                    P_T[:, kt, :],
                                    start=(kt == 0), stop=(kt == qt))
                            return inst
                        P.op("tensor", mkpv)
                    P.phase()
            for dh in range(2):
                tgt = P.dma("sync", "dT", lambda e, dh=dh: e.dma_start(
                    out=wot,
                    in_=scr[3].rearrange("(f p) m -> p f m", p=128)[
                        :, :, dh * 512:(dh + 1) * 512]))
                P.phase()
                P.need("dT", tgt)
                for tl in range(4):
                    def mko(e, tl=tl, dh=dh):
                        inst = None
                        for fc in range(FC):
                            inst = e.matmul(
                                self.pb[tl], O_q[:, fc, tl * 128:(tl + 1) * 128],
                                wot[:, fc, :], start=(fc == 0), stop=(fc == FC - 1))
                        return inst
                    P.op("tensor", mko)
                P.phase()
                for tl in range(4):
                    t = quar * 4 + tl
                    xs = self.X[:, t * D + dh * 512: t * D + (dh + 1) * 512]
                    P.op("vector", lambda e, xs=xs, tl=tl: e.tensor_tensor(
                        xs, xs, self.pb[tl], ALU.add))
                P.phase()
        self.abf.pop()

    def _proj_fm(self, dst, wt, dma_tgt, scale=1.0, tqs=None, toff=0):
        P = self.P
        P.need("dT", dma_tgt)
        if tqs is None:
            tqs = range(T // 512)
        groups = []
        for oc in range(FC):
            for tq in tqs:
                d0 = tq * 512 - toff
                def mk(e, p, oc=oc, tq=tq):
                    inst = None
                    for ic in range(FC):
                        inst = e.matmul(p, wt[:, ic, oc * 128:(oc + 1) * 128],
                                        self.xn_chunk(ic)[:, tq * 512:(tq + 1) * 512],
                                        start=(ic == 0), stop=(ic == FC - 1))
                    return inst
                def ev(e, p, oc=oc, d0=d0):
                    return self._evac(e, dst[:, oc, d0:d0 + 512], p, scale)
                groups.append((mk, ev))
        self.mm_waves(groups)

    def _proj_tok(self, dst, wt, dma_tgt):
        P = self.P
        P.need("dT", dma_tgt)
        groups = []
        for t in range(TT):
            for oh in range(2):
                def mk(e, p, t=t, oh=oh):
                    inst = None
                    for ic in range(FC):
                        inst = e.matmul(p, self.xn_chunk(ic)[:, t * 128:(t + 1) * 128],
                                        wt[:, ic, oh * 512:(oh + 1) * 512],
                                        start=(ic == 0), stop=(ic == FC - 1))
                    return inst
                def ev(e, p, t=t, oh=oh):
                    return self._evac(e, dst[:, t, oh * 512:(oh + 1) * 512], p)
                groups.append((mk, ev))
        self.mm_waves(groups)


def host_consts():
    qq = np.arange(128)[:, None]
    c = np.arange(256)[None, :]
    allowed = np.where(c < 128, c > qq, (c - 128) <= qq)
    mask_w = np.where(allowed, 0.0, NEG).astype(np.float32)
    maskpad = np.zeros((128, 1024), np.float32)
    maskpad[:, 384:512] = mask_w[:, 128:256]
    mask256 = np.concatenate([mask_w[:, 128:256],
                              np.full((128, 128), NEG, np.float32)], axis=1)
    return {
        "ident_bf": np.eye(128, dtype=np.float32).astype(bfdt),
        "ident_f32": np.eye(128, dtype=np.float32),
        "mask_w": mask_w.astype(bfdt),
        "mask256": mask256.astype(bfdt),
        "maskpad": maskpad.astype(bfdt),
        "ones_bf": np.ones((128, 128), np.float32).astype(bfdt),
        "ones_f32": np.ones((1, 128), np.float32),
        "onec_f32": np.ones((128, 1), np.float32),
    }


CONST_SPECS = {
    "ident_bf": ([128, 128], BF16),
    "ident_f32": ([128, 128], F32),
    "mask_w": ([128, 256], BF16),
    "mask256": ([128, 256], BF16),
    "maskpad": ([128, 1024], BF16),
    "ones_bf": ([128, 128], BF16),
    "ones_f32": ([1, 128], F32),
    "onec_f32": ([128, 1], F32),
}


def build_sl2_launch():
    nc = bass.Bass("TRN2", target_bir_lowering=False)
    es = ExitStack()
    xin = nc.dram_tensor("xin", [T, D], F32, kind="ExternalInput").ap()
    xout = nc.dram_tensor("xout", [T, D], F32, kind="ExternalOutput").ap()
    w_fc = nc.dram_tensor("w_fc", [4096, D], F32, kind="ExternalInput").ap()
    w_proj = nc.dram_tensor("w_proj", [D, 4096], F32, kind="ExternalInput").ap()
    u = nc.dram_tensor("u", [4096, 16], F32, kind="ExternalInput").ap()
    v = nc.dram_tensor("v", [16, D], F32, kind="ExternalInput").ap()
    fb = nc.dram_tensor("fb", [16, 16], F32, kind="ExternalInput").ap()
    gate = nc.dram_tensor("gate", [1, 1], F32, kind="ExternalInput").ap()
    scr_fc = nc.dram_tensor("scr_fc", [4096, D], BF16, kind="Internal").ap()
    scr_pr = nc.dram_tensor("scr_pr", [D, 4096], BF16, kind="Internal").ap()
    cmap = {}
    for name, (shape, dt) in CONST_SPECS.items():
        cmap[name] = nc.dram_tensor("c_" + name, shape, dt, kind="ExternalInput").ap()
    b = B(nc, es)
    b.load_consts(cmap)
    b.load_x(xin)
    b.sl2_prep(w_fc, w_proj, u, v, fb, gate, scr_fc, scr_pr)
    b.sl2(scr_fc, scr_pr, first=True)
    b.store_x(xout)
    b.P.replay()
    es.close()
    return nc



SEQ = ["sl1", "sl2", "sl3", "sl2b", "sl5", "sl6", "sl7", "sl6b"]


def build_launch(ids):
    nc = bass.Bass("TRN2", target_bir_lowering=False)
    es = ExitStack()
    xin = nc.dram_tensor("xin", [T, D], F32, kind="ExternalInput").ap()
    xout = nc.dram_tensor("xout", [T, D], F32, kind="ExternalOutput").ap()
    names = set(SEQ[i] for i in ids)
    dr = {}

    def din(name, shape):
        dr[name] = nc.dram_tensor(name, shape, F32, kind="ExternalInput").ap()
        return dr[name]

    def scrt(name, shape):
        return nc.dram_tensor(name, shape, BF16, kind="Internal").ap()

    if "sl1" in names:
        for nm, sh in (("L0_attn_a_Wq_T", [D, 4096]), ("L0_attn_a_bq", [4096]),
                       ("L0_attn_a_Wk_T", [D, 512]), ("L0_attn_a_bk", [512]),
                       ("L0_attn_a_Wv_T", [D, 512]), ("L0_attn_a_bv", [512]),
                       ("L0_attn_a_Wo_T", [4096, D]), ("L0_attn_a_bo", [D]),
                       ("L0_attn_a_sinks", [64])):
            din(nm, sh)
        scr1 = {"q": scrt("s1q", [D, 4096]), "k": scrt("s1k", [D, 512]),
                "v": scrt("s1v", [D, 512]), "o": scrt("s1o", [4096, D])}
    if "sl2" in names or "sl2b" in names:
        for nm, sh in (("L0_ffn_w_fc", [4096, D]), ("L0_ffn_w_proj", [D, 4096]),
                       ("L0_ffn_u", [4096, 16]), ("L0_ffn_v", [16, D]),
                       ("L0_ffn_fast_b", [16, 16]), ("L0_ffn_gate", [1, 1])):
            din(nm, sh)
        scr2fc = scrt("s2fcT", [D, 4096])
        scr2pr = scrt("s2prT", [4096, D])
    if "sl3" in names:
        for nm, sh in (("L0_attn_b_W_Q_T", [D, D]), ("L0_attn_b_W_K_T", [D, D]),
                       ("L0_attn_b_W_V_T", [D, D]), ("L0_attn_b_W_O_T", [D, D])):
            din(nm, sh)
        scr3 = [scrt(f"s3{i}", [D, D]) for i in range(4)]
    if "sl5" in names:
        for nm, sh in (("L1_attn_a_Wqkv", [D, 3 * D]), ("L1_attn_a_Wout", [D, D]),
                       ("L1_attn_a_gamma", [D]), ("L1_attn_a_beta", [D])):
            din(nm, sh)
        scr5q = scrt("s5q", [D, 3 * D])
        scr5o = scrt("s5o", [D, D])
    if "sl6" in names or "sl6b" in names:
        for nm, sh in (("L1_ffn_sim_matrix", [D, 16]), ("L1_ffn_gates", [16]),
                       ("L1_ffn_q_proj", [16 * D, 128]),
                       ("L1_ffn_k_proj", [16 * D, 128]),
                       ("L1_ffn_v_proj", [16 * D, 128]),
                       ("L1_ffn_o_proj", [16 * 128, D])):
            din(nm, sh)
        scr6 = {nm: scrt(f"s6{nm}", [16 * D, 128]) for nm in ("q", "k", "v")}
        scr6["o"] = scrt("s6o", [16 * 128, D])
    if "sl7" in names:
        for nm in ("L1_attn_b_Wq_T", "L1_attn_b_Wk_T", "L1_attn_b_Wv_T"):
            din(nm, [D, D])
        scr7 = [scrt(f"s7{i}", [D, D]) for i in range(3)]

    cmap = {}
    for name, (shape, dt) in CONST_SPECS.items():
        cmap[name] = nc.dram_tensor("c_" + name, shape, dt,
                                    kind="ExternalInput").ap()
    b = B(nc, es)
    b.load_consts(cmap)
    b.load_x(xin)
    tiles2 = None
    prep6 = None
    first = True
    for i in ids:
        sl = SEQ[i]
        if sl == "sl1":
            b.sl1(dr["L0_attn_a_Wq_T"], dr["L0_attn_a_bq"], dr["L0_attn_a_Wk_T"],
                  dr["L0_attn_a_bk"], dr["L0_attn_a_Wv_T"], dr["L0_attn_a_bv"],
                  dr["L0_attn_a_Wo_T"], dr["L0_attn_a_bo"], dr["L0_attn_a_sinks"],
                  scr1, first=first)
        elif sl in ("sl2", "sl2b"):
            if tiles2 is None:
                b.sl2_prep(dr["L0_ffn_w_fc"], dr["L0_ffn_w_proj"],
                           dr["L0_ffn_u"], dr["L0_ffn_v"],
                           dr["L0_ffn_fast_b"], dr["L0_ffn_gate"],
                           scr2fc, scr2pr)
                tiles2 = True
            b.sl2(scr2fc, scr2pr, first=first)
        elif sl == "sl3":
            b.sl3(dr["L0_attn_b_W_Q_T"], dr["L0_attn_b_W_K_T"],
                  dr["L0_attn_b_W_V_T"], dr["L0_attn_b_W_O_T"], scr3,
                  first=first)
        elif sl == "sl5":
            b.sl5(dr["L1_attn_a_Wqkv"], dr["L1_attn_a_Wout"],
                  dr["L1_attn_a_gamma"], dr["L1_attn_a_beta"],
                  scr5q, scr5o, first=first)
        elif sl in ("sl6", "sl6b"):
            if prep6 is None:
                prep6 = b.sl6_prep(dr["L1_ffn_sim_matrix"], dr["L1_ffn_gates"],
                                   dr["L1_ffn_q_proj"], dr["L1_ffn_k_proj"],
                                   dr["L1_ffn_v_proj"], dr["L1_ffn_o_proj"],
                                   scr6)
            b.sl6(prep6[0], prep6[1], scr6, first=first)
        elif sl == "sl7":
            b.sl7(dr["L1_attn_b_Wq_T"], dr["L1_attn_b_Wk_T"],
                  dr["L1_attn_b_Wv_T"], scr7)
        first = False
    b.store_x(xout)
    b.P.replay()
    es.close()
    return nc, sorted(dr.keys())


LAUNCH_GROUPS = [[0, 1, 2, 3, 4, 5, 6, 7]]

_BUILD_CACHE = {}


def kernel(**inputs):
    consts = host_consts()
    x = np.ascontiguousarray(inputs["L0_attn_a_x"], dtype=np.float32)
    nco = 8
    reshape = {
        "L0_attn_b_W_Q": (D, D), "L0_attn_b_W_K": (D, D),
        "L0_attn_b_W_V": (D, D),
        "L1_ffn_q_proj": (16 * D, 128), "L1_ffn_k_proj": (16 * D, 128),
        "L1_ffn_v_proj": (16 * D, 128), "L1_ffn_o_proj": (16 * 128, D),
        "L0_ffn_gate": (1, 1),
    }
    cur = [x[c] for c in range(nco)]
    for gi, ids in enumerate(LAUNCH_GROUPS):
        key = tuple(ids)
        if key not in _BUILD_CACHE:
            _BUILD_CACHE[key] = build_launch(ids)
        nc, wnames = _BUILD_CACHE[key]
        tbase = {
            "L0_attn_a_Wq_T": "L0_attn_a_Wq", "L0_attn_a_Wk_T": "L0_attn_a_Wk",
            "L0_attn_a_Wv_T": "L0_attn_a_Wv", "L0_attn_a_Wo_T": "L0_attn_a_Wo",
            "L0_attn_b_W_Q_T": "L0_attn_b_W_Q",
            "L0_attn_b_W_K_T": "L0_attn_b_W_K",
            "L0_attn_b_W_V_T": "L0_attn_b_W_V",
            "L0_attn_b_W_O_T": "L0_attn_b_W_O",
            "L1_attn_b_Wq_T": "L1_attn_b_Wq", "L1_attn_b_Wk_T": "L1_attn_b_Wk",
            "L1_attn_b_Wv_T": "L1_attn_b_Wv",
        }
        shared = {}
        for nm in wnames:
            base = tbase[nm] if nm.endswith("_T") else nm
            a = np.asarray(inputs[base], dtype=np.float32)
            if base in reshape:
                a = a.reshape(reshape[base])
            if nm.endswith("_T"):
                a = a.T
            shared[nm] = np.ascontiguousarray(a)
        in_maps = []
        for c in range(nco):
            m = {"xin": np.ascontiguousarray(cur[c])}
            m.update(shared)
            for kk, vv in consts.items():
                m["c_" + kk] = vv
            in_maps.append(m)
        res = run_bass_kernel_spmd(nc, in_maps, core_ids=list(range(nco)))
        cur = [r["xout"] for r in res.results]
    return np.stack(cur, 0).astype(np.float32)


def build_sl1_launch():
    nc = bass.Bass("TRN2", target_bir_lowering=False)
    es = ExitStack()
    xin = nc.dram_tensor("xin", [T, D], F32, kind="ExternalInput").ap()
    xout = nc.dram_tensor("xout", [T, D], F32, kind="ExternalOutput").ap()
    wq = nc.dram_tensor("wq", [4096, D], F32, kind="ExternalInput").ap()
    bq = nc.dram_tensor("bq", [4096], F32, kind="ExternalInput").ap()
    wk = nc.dram_tensor("wk", [512, D], F32, kind="ExternalInput").ap()
    bk = nc.dram_tensor("bk", [512], F32, kind="ExternalInput").ap()
    wv = nc.dram_tensor("wv", [512, D], F32, kind="ExternalInput").ap()
    bv = nc.dram_tensor("bv", [512], F32, kind="ExternalInput").ap()
    wo = nc.dram_tensor("wo", [D, 4096], F32, kind="ExternalInput").ap()
    bo = nc.dram_tensor("bo", [D], F32, kind="ExternalInput").ap()
    sinks = nc.dram_tensor("sinks", [64], F32, kind="ExternalInput").ap()
    scr = {"q": nc.dram_tensor("scr_q", [4096, D], BF16, kind="Internal").ap(),
           "k": nc.dram_tensor("scr_k", [512, D], BF16, kind="Internal").ap(),
           "v": nc.dram_tensor("scr_v", [512, D], BF16, kind="Internal").ap(),
           "o": nc.dram_tensor("scr_o", [D, 4096], BF16, kind="Internal").ap()}
    cmap = {}
    for name, (shape, dt) in CONST_SPECS.items():
        cmap[name] = nc.dram_tensor("c_" + name, shape, dt, kind="ExternalInput").ap()
    b = B(nc, es)
    b.load_consts(cmap)
    b.load_x(xin)
    b.sl1(wq, bq, wk, bk, wv, bv, wo, bo, sinks, scr, first=True)
    b.store_x(xout)
    b.P.replay()
    es.close()
    return nc


def build_sl6_launch():
    nc = bass.Bass("TRN2", target_bir_lowering=False)
    es = ExitStack()
    xin = nc.dram_tensor("xin", [T, D], F32, kind="ExternalInput").ap()
    xout = nc.dram_tensor("xout", [T, D], F32, kind="ExternalOutput").ap()
    sim_m = nc.dram_tensor("sim_m", [D, 16], F32, kind="ExternalInput").ap()
    gates = nc.dram_tensor("gates", [16], F32, kind="ExternalInput").ap()
    wq = nc.dram_tensor("wq", [16 * D, 128], F32, kind="ExternalInput").ap()
    wk = nc.dram_tensor("wk", [16 * D, 128], F32, kind="ExternalInput").ap()
    wv = nc.dram_tensor("wv", [16 * D, 128], F32, kind="ExternalInput").ap()
    wo = nc.dram_tensor("wo", [16 * 128, D], F32, kind="ExternalInput").ap()
    scr = {nm: nc.dram_tensor(f"scr_{nm}", [16 * D, 128], BF16, kind="Internal").ap()
           for nm in ("q", "k", "v")}
    scr["o"] = nc.dram_tensor("scr_o", [16 * 128, D], BF16, kind="Internal").ap()
    cmap = {}
    for name, (shape, dt) in CONST_SPECS.items():
        cmap[name] = nc.dram_tensor("c_" + name, shape, dt, kind="ExternalInput").ap()
    b = B(nc, es)
    b.load_consts(cmap)
    b.load_x(xin)
    simn, sgbc = b.sl6_prep(sim_m, gates, wq, wk, wv, wo, scr)
    b.sl6(simn, sgbc, scr, first=True)
    b.store_x(xout)
    b.P.replay()
    es.close()
    return nc


def build_sl5_launch():
    nc = bass.Bass("TRN2", target_bir_lowering=False)
    es = ExitStack()
    xin = nc.dram_tensor("xin", [T, D], F32, kind="ExternalInput").ap()
    xout = nc.dram_tensor("xout", [T, D], F32, kind="ExternalOutput").ap()
    wqkv = nc.dram_tensor("wqkv", [D, 3 * D], F32, kind="ExternalInput").ap()
    wout = nc.dram_tensor("wout", [D, D], F32, kind="ExternalInput").ap()
    gamma = nc.dram_tensor("gamma", [D], F32, kind="ExternalInput").ap()
    beta = nc.dram_tensor("beta", [D], F32, kind="ExternalInput").ap()
    scr_qkv = nc.dram_tensor("scr_qkv", [D, 3 * D], BF16, kind="Internal").ap()
    scr_out = nc.dram_tensor("scr_out", [D, D], BF16, kind="Internal").ap()
    cmap = {}
    for name, (shape, dt) in CONST_SPECS.items():
        cmap[name] = nc.dram_tensor("c_" + name, shape, dt, kind="ExternalInput").ap()
    b = B(nc, es)
    b.load_consts(cmap)
    b.load_x(xin)
    b.sl5(wqkv, wout, gamma, beta, scr_qkv, scr_out, first=True,
          upto=getattr(B, "DBG_SL5_UPTO", "full") or "full")
    b.store_x(xout)
    b.P.replay()
    es.close()
    return nc


def build_sl3_launch():
    nc = bass.Bass("TRN2", target_bir_lowering=False)
    es = ExitStack()
    xin = nc.dram_tensor("xin", [T, D], F32, kind="ExternalInput").ap()
    xout = nc.dram_tensor("xout", [T, D], F32, kind="ExternalOutput").ap()
    wq = nc.dram_tensor("wq", [D, D], F32, kind="ExternalInput").ap()
    wk = nc.dram_tensor("wk", [D, D], F32, kind="ExternalInput").ap()
    wv = nc.dram_tensor("wv", [D, D], F32, kind="ExternalInput").ap()
    wo = nc.dram_tensor("wo", [D, D], F32, kind="ExternalInput").ap()
    scr = [nc.dram_tensor(f"scr{i}", [D, D], BF16, kind="Internal").ap()
           for i in range(4)]
    cmap = {}
    for name, (shape, dt) in CONST_SPECS.items():
        cmap[name] = nc.dram_tensor("c_" + name, shape, dt, kind="ExternalInput").ap()
    b = B(nc, es)
    b.load_consts(cmap)
    b.load_x(xin)
    b.sl3(wq, wk, wv, wo, scr, first=True)
    b.store_x(xout)
    b.P.replay()
    es.close()
    return nc


def build_sl7_launch(upto="full"):
    nc = bass.Bass("TRN2", target_bir_lowering=False)
    es = ExitStack()
    xin = nc.dram_tensor("xin", [T, D], F32, kind="ExternalInput").ap()
    xout = nc.dram_tensor("xout", [T, D], F32, kind="ExternalOutput").ap()
    wq = nc.dram_tensor("wq", [D, D], F32, kind="ExternalInput").ap()
    wk = nc.dram_tensor("wk", [D, D], F32, kind="ExternalInput").ap()
    wv = nc.dram_tensor("wv", [D, D], F32, kind="ExternalInput").ap()
    scr = [nc.dram_tensor(f"scr{i}", [D, D], BF16, kind="Internal").ap()
           for i in range(3)]
    cmap = {}
    for name, (shape, dt) in CONST_SPECS.items():
        cmap[name] = nc.dram_tensor("c_" + name, shape, dt, kind="ExternalInput").ap()
    b = B(nc, es)
    b.load_consts(cmap)
    b.load_x(xin)
    b.sl7(wq, wk, wv, scr, upto=upto)
    b.store_x(xout)
    b.P.replay()
    es.close()
    return nc


